# version 15
# baseline (speedup 1.0000x reference)
import numpy as np

import concourse.bass as bass
import concourse.mybir as mybir
from concourse.bass_utils import run_bass_kernel_spmd

F32 = mybir.dt.float32
BF16 = mybir.dt.bfloat16
AF = mybir.ActivationFunctionType
ALU = mybir.AluOpType
AX = mybir.AxisListType

B, S, D = 2, 4096, 2048
TOK = 1024
DC = D // 128
DFF = 1408
FC = DFF // 128
NSH, NR, NE = 2, 16, 18
KR = 4
NH, NKV = 16, 4
D_NOPE, D_ROPE, D_QK, D_V = 128, 64, 192, 128
LORA = 512
QH = NH * D_QK
SCALE = float(1.0 / np.sqrt(D_QK))
EPS = 1e-5
NEG = -1.0e30

DEBUG = dict(n_groups=8, do_attn=True, do_ffn2=True, kvh_list=(0, 1, 2, 3),
             mock_collective=False)


class Sched:
    COMPUTE = ("tensor", "scalar", "vector", "gpsimd")
    QUEUES = ("sync", "gpsimd", "scalar")
    NDMA = 12

    def __init__(self, nc):
        self.nc = nc
        self.q = {e: [] for e in ("tensor", "scalar", "vector", "gpsimd", "sync")}
        self.prog_sem = {e: nc.alloc_semaphore(f"pg_{e}") for e in self.COMPUTE}
        self.prog_cnt = {e: 0 for e in self.COMPUTE}
        self.dma_sem = {}
        self.dma_cnt = {}
        for qn in self.QUEUES:
            self.dma_sem[qn] = [nc.alloc_semaphore(f"dm_{qn}{i}")
                                for i in range(self.NDMA)]
            self.dma_cnt[qn] = 0
        self.cc_sem = nc.alloc_semaphore("cc")
        self.cc_cnt = 0
        self.res_w = {}
        self.res_r = {}
        self.sat = {e: {} for e in self.q}

    def _tok_wait(self, tok):
        if tok[0] == 'p':
            return (self.prog_sem[tok[1]], tok[2])
        if tok[0] == 'c':
            return (self.cc_sem, tok[1])
        qn, idx = tok[1], tok[2]
        return (self.dma_sem[qn][idx % self.NDMA], 16 * (idx // self.NDMA + 1))

    def _need(self, eng, tok):
        sem, val = self._tok_wait(tok)
        key = sem.num
        if val <= self.sat[eng].get(key, -1):
            return None
        self.sat[eng][key] = val
        return (sem, val)

    def op(self, eng, fn, reads=(), writes=(), dma=False, acc=None, coll=False):
        deps = []
        if dma and self.dma_cnt[eng] >= self.NDMA:
            deps.append(('d', eng, self.dma_cnt[eng] - self.NDMA))
        for r in reads:
            t = self.res_w.get(r)
            if t is not None:
                deps.append(t)
        if acc in (None, 'start'):
            for w in writes:
                t = self.res_w.get(w)
                if t is not None:
                    deps.append(t)
                deps.extend(self.res_r.get(w, ()))
        waits = []
        for t in deps:
            wv = self._need(eng, t)
            if wv:
                waits.append(wv)
        if coll:
            self.cc_cnt += 1
            tok = ('c', self.cc_cnt)
            inc = (self.cc_sem, None)
        elif dma:
            idx = self.dma_cnt[eng]
            self.dma_cnt[eng] += 1
            tok = ('d', eng, idx)
            inc = (self.dma_sem[eng][idx % self.NDMA], 16)
        else:
            self.prog_cnt[eng] += 1
            tok = ('p', eng, self.prog_cnt[eng])
            inc = (self.prog_sem[eng], 1)
        self.q[eng].append((waits, fn, inc))
        for r in reads:
            self.res_r.setdefault(r, []).append(tok)
        if acc in (None, 'stop'):
            for w in writes:
                self.res_w[w] = tok
                self.res_r[w] = []
        return tok

    def frontier(self):
        toks = [('p', e, self.prog_cnt[e]) for e in self.COMPUTE
                if self.prog_cnt[e] > 0]
        for q in self.QUEUES:
            n = self.dma_cnt[q]
            for slot in range(min(n, self.NDMA)):
                nslot = (n - 1 - slot) // self.NDMA + 1
                toks.append(('d', q, slot + (nslot - 1) * self.NDMA))
        if self.cc_cnt > 0:
            toks.append(('c', self.cc_cnt))
        return toks

    def barrier(self):
        toks = self.frontier()
        for e in self.q:
            waits = []
            for t in toks:
                if t[0] == 'p' and t[1] == e:
                    continue
                wv = self._need(e, t)
                if wv:
                    waits.append(wv)
            if waits:
                self.q[e].append((waits, None, None))

    def finalize(self):
        waits = []
        for t in self.frontier():
            wv = self._need('sync', t)
            if wv:
                waits.append(wv)
        for qn in self.QUEUES:
            n = self.dma_cnt[qn]
            for slot in range(min(n, self.NDMA)):
                nslot = (n - 1 - slot) // self.NDMA + 1
                wv = self._need('sync', ('d', qn, slot + (nslot - 1) * self.NDMA))
                if wv:
                    waits.append(wv)
        if waits:
            self.q['sync'].append((waits, None, None))

    def run(self, block):
        def runner(name):
            def go(eng):
                for waits, fn, inc in self.q[name]:
                    for sem, val in waits:
                        eng.wait_ge(sem, val)
                    if fn is not None:
                        ins = fn(eng)
                        if inc is not None:
                            if inc[1] is None:
                                ins.then_inc(inc[0])
                            else:
                                ins.then_inc(inc[0], inc[1])
            return go
        block.tensor(runner('tensor'))
        block.scalar(runner('scalar'))
        block.vector(runner('vector'))
        block.gpsimd(runner('gpsimd'))
        block.sync(runner('sync'))


def build_nc(dbg=DEBUG):
    nc = bass.Bass("TRN2", target_bir_lowering=False)
    s = Sched(nc)
    NGRP = dbg.get("n_groups", 8)

    d_x = nc.dram_tensor("x", [D, TOK], F32, kind="ExternalInput").ap()
    d_wqkv = nc.dram_tensor("wqkv", [28, 128, 16, 128], F32, kind="ExternalInput").ap()
    d_wk = nc.dram_tensor("wk", [NKV, 128, 4, 192], F32, kind="ExternalInput").ap()
    d_wv = nc.dram_tensor("wv", [NKV, 128, 4, 128], F32, kind="ExternalInput").ap()
    d_wo = nc.dram_tensor("wo", [16, 128, 16, 128], F32, kind="ExternalInput").ap()
    d_w1 = nc.dram_tensor("w1", [NE, FC, 128, DC, 128], F32, kind="ExternalInput").ap()
    d_w2 = nc.dram_tensor("w2", [NE, DC, 128, FC, 128], F32, kind="ExternalInput").ap()
    d_b1 = nc.dram_tensor("b1", [128, NE, FC], F32, kind="ExternalInput").ap()
    d_b2sh = nc.dram_tensor("b2sh", [NSH, D], F32, kind="ExternalInput").ap()
    d_b2r = nc.dram_tensor("b2r", [NR, D], F32, kind="ExternalInput").ap()
    d_cent = nc.dram_tensor("cent", [128, DC, NR], F32, kind="ExternalInput").ap()
    d_gbias = nc.dram_tensor("gbias", [1, NR], F32, kind="ExternalInput").ap()
    d_qcos = nc.dram_tensor("qcos", [D_ROPE, TOK], F32, kind="ExternalInput").ap()
    d_qsin = nc.dram_tensor("qsin", [D_ROPE, TOK], F32, kind="ExternalInput").ap()
    d_kcos = nc.dram_tensor("kcos", [D_ROPE, S], F32, kind="ExternalInput").ap()
    d_ksin = nc.dram_tensor("ksin", [D_ROPE, S], F32, kind="ExternalInput").ap()
    d_mask = nc.dram_tensor("mask", [128, 512], F32, kind="ExternalInput").ap()
    d_maskt = nc.dram_tensor("maskt", [4, 128, 128], F32, kind="ExternalInput").ap()
    d_ident = nc.dram_tensor("ident", [128, 128], F32, kind="ExternalInput").ap()
    d_out = nc.dram_tensor("out", [D, TOK], F32, kind="ExternalOutput").ap()

    d_qspill = nc.dram_tensor("qspill", [QH, TOK], F32, kind="Internal").ap()
    d_ospill = nc.dram_tensor("ospill", [D, TOK], BF16, kind="Internal").ap()
    d_kswap = nc.dram_tensor("kswap", [D_ROPE, S], BF16, kind="Internal").ap()
    d_gspill = nc.dram_tensor("gspill", [NR, TOK], BF16, kind="Internal").ap()
    d_cin = nc.dram_tensor("cin", [LORA, TOK], BF16, kind="Internal").ap()
    d_cout = nc.dram_tensor("cout", [4 * LORA, TOK], BF16, kind="Internal").ap()

    T = {}

    def sb(name, shape, dt):
        T[name] = nc.alloc_sbuf_tensor(name, shape, dt)

    for i in range(DC):
        sb(f"x{i}", [128, TOK], F32)
    sb("ident_bf", [128, 128], BF16)
    sb("ident_f", [128, 128], F32)
    sb("onesc_f", [128, 1], F32)
    sb("onesc_b", [128, 1], BF16)
    sb("ones1_f", [1, 128], F32)
    sb("ones1_b", [1, 128], BF16)
    sb("mcol", [128, 8], F32)
    sb("rscol", [128, 8], F32)
    sb("smallc", [128, 8], F32)
    sb("csum_b", [128, NR], F32)
    sb("biasr_b", [128, NR], F32)
    sb("csum_r", [1, NR], F32)
    sb("bm0", [128, 8], F32)
    sb("bm1", [128, 8], F32)
    sb("sx0", [128, 8], F32)
    sb("sx1", [128, 8], F32)
    sb("rcp0", [128, 8], F32)
    sb("rcp1", [128, 8], F32)
    sb("rci", [128, 4], F32)
    sb("nmx0", [128, 8], BF16)
    sb("nmx1", [128, 8], BF16)
    sb("gbias_r", [1, NR], F32)

    marker = nc.alloc_sbuf_tensor("ovl_marker", [128, 16], F32)
    ovl_base = nc.lookup_mloc("ovl_marker").addr + 64

    def ov(name, shape, dt, off):
        T[name] = nc.alloc_sbuf_tensor_at(name, shape, dt, offset=ovl_base + off)

    o = 0
    for i in range(DC):
        ov(f"xn{i}", [128, TOK], BF16, o + 2048 * i)
    o += 2048 * DC
    for b in range(2):
        for fb in range(FC):
            ov(f"h{b}_{fb}", [128, TOK], BF16, o + (b * FC + fb) * 2048)
    o += 2 * FC * 2048
    ov("xsq0", [128, TOK], BF16, o); o += 2048
    ov("xsq1", [128, TOK], BF16, o); o += 2048
    ov("gb0", [128, TOK], BF16, o); o += 2048
    ov("gb1", [128, TOK], BF16, o); o += 2048
    ov("m_b", [128, TOK], F32, o); o += 4096
    ov("rs_b", [128, TOK], F32, o); o += 4096
    ov("xtmp", [128, TOK], F32, o); o += 4096
    ov("m_row", [1, TOK], F32, o); o += 4096
    ov("rs_row", [1, TOK], F32, o); o += 4096
    NW1 = 3
    for b in range(NW1):
        ov(f"w1s{b}", [128, 2048], BF16, o + b * 4096)
    o += NW1 * 4096
    NW2 = 3
    for b in range(NW2):
        ov(f"w2s{b}", [128, FC * 128], BF16, o + b * 2816)
    o += NW2 * 2816
    ov("b2sh", [NSH, D], BF16, o); o += 4096
    ov("b2r", [NR, D], BF16, o); o += 4096
    ov("ones2", [NSH, TOK], BF16, o); o += 2048
    ov("b1t", [128, NE * FC], F32, o); o += 800
    ov("cent", [128, DC * NR], F32, o); o += 1024
    ov("G_bf", [NR, TOK], BF16, o); o += 2048
    ov("g_row", [1, TOK], BF16, o); o += 2048
    ov("Dt", [128, NR], F32, o); o += 64
    ov("sc0", [128, NR], F32, o); o += 64
    ov("sc1", [128, NR], F32, o); o += 64
    ov("w0", [128, NR], F32, o); o += 64
    ov("w1t", [128, NR], F32, o); o += 64
    ov("gat", [128, NR], F32, o); o += 64
    ov("gat_b", [128, NR], BF16, o); o += 32
    ffn_end = o

    ov("S0", [128, S], F32, 0)
    ov("S1", [128, S], F32, 16384)
    NPT = 8
    for p_ in range(NPT):
        ov(f"pt{p_}", [128, 512], BF16, 32768 + p_ * 1024)
    ov("ks", [D_ROPE, S], BF16, 40960)
    ov("rcb", [128, 512], F32, 49152)
    ov("nmrow", [1, 512], BF16, 51200)
    ov("rcrow", [1, 512], F32, 52224)
    ov("srow", [1, 512], F32, 56320)
    ov("maskt", [128, 512], F32, 54272)
    for c in range(4):
        ov(f"ckvl{c}", [128, TOK], BF16, 65536 + c * 2048)
    ov("qst0", [128, TOK], F32, 73728)
    ov("qst1", [128, TOK], F32, 77824)
    ov("kn", [128, S], BF16, 65536)
    ov("krr", [D_ROPE + 1, S], BF16, 73728)
    ov("cs0", [128, 2048], BF16, 81920)
    ov("cs1", [128, 2048], BF16, 86016)
    ov("Vh", [128, S], BF16, 90112)
    a = 98304
    ov("qn0", [128, TOK], BF16, a); a += 2048
    ov("qn1", [128, TOK], BF16, a); a += 2048
    ov("qr0", [D_ROPE + 1, TOK], BF16, a); a += 2048
    ov("qr1", [D_ROPE + 1, TOK], BF16, a); a += 2048
    ov("qs", [D_ROPE, TOK], BF16, a); a += 2048
    ov("kcos", [D_ROPE, S], BF16, a); a += 8192
    ov("ksin", [D_ROPE, S], BF16, a); a += 8192
    ov("qcos", [D_ROPE, TOK], BF16, a); a += 2048
    ov("qsin", [D_ROPE, TOK], BF16, a); a += 2048
    ov("maskf", [128, 512], F32, a); a += 2048
    for b in range(2):
        ov(f"wks{b}", [128, 4 * 192], BF16, a + b * 1536)
    a += 2 * 1536
    for b in range(2):
        ov(f"wvs{b}", [128, 4 * 128], BF16, a + b * 1024)
    a += 2 * 1024
    for b in range(2):
        ov(f"ost{b}", [128, 512], BF16, a + b * 1024)
    a += 2048
    ov("gmx0", [128, 512], F32, a); a += 2048
    ov("gmx1", [128, 512], F32, a); a += 2048
    for kb in range(16):
        ov(f"Ores{kb}", [128, TOK], BF16, kb * 2048)
    for b in range(2):
        ov(f"wos{b}", [128, 2048], BF16, 32768 + b * 4096)
    attn_end = a

    ovl_size = max(ffn_end, attn_end)
    assert ovl_base + ovl_size <= 224 * 1024 - 256, \
        f"SBUF overflow: base={ovl_base} size={ovl_size}"

    P = {}
    P['hp0'] = nc.alloc_psum_tensor("hp0", [128, 1024], F32)
    P['hp1'] = nc.alloc_psum_tensor("hp1", [128, 1024], F32)
    P['yp0'] = nc.alloc_psum_tensor("yp0", [128, 512], F32)
    P['yp1'] = nc.alloc_psum_tensor("yp1", [128, 512], F32)
    P['gp'] = nc.alloc_psum_tensor("gp", [128, 1024], F32)
    gp_bf = P['gp'][:].bitcast(BF16)

    def hpk(b, half):
        return f"PB{b * 2 + half}"
    YP0, YP1, GPA, GPB = "PB4", "PB5", "PB6", "PB7"

    def dma(qname, fn, reads=(), writes=()):
        return s.op(qname, fn, reads=reads, writes=writes, dma=True)

    def op(eng, fn, reads=(), writes=(), acc=None):
        return s.op(eng, fn, reads=reads, writes=writes, acc=acc)

    def acctag(i, n):
        return 'start' if i == 0 else ('stop' if i == n - 1 else 'mid')

    for i in range(DC):
        dma('sync', (lambda t_, a_: lambda e: e.dma_start(t_[:], a_))(
            T[f"x{i}"], d_x[128 * i:128 * (i + 1), :]), writes=[f"x{i}"])
    dma('gpsimd', lambda e: e.dma_start(T["ident_bf"][:], d_ident), writes=["ident_bf"])
    dma('sync', lambda e: e.dma_start(T["ident_f"][:], d_ident), writes=["ident_f"])
    op('vector', lambda e: e.memset(T["onesc_f"][:], 1.0), writes=["onesc_f"])
    op('vector', lambda e: e.memset(T["onesc_b"][:], 1.0), writes=["onesc_b"])
    op('vector', lambda e: e.memset(T["ones1_f"][:], 1.0), writes=["ones1_f"])
    op('vector', lambda e: e.memset(T["ones1_b"][:], 1.0), writes=["ones1_b"])

    def ffn_const_dmas():
        dma('sync', lambda e: e.dma_start(T["cent"][:],
                                          d_cent.rearrange("p c e -> p (c e)")),
            writes=["cent"])
        dma('sync', lambda e: e.dma_start(T["b1t"][:, 0:NE * FC],
                                          d_b1.rearrange("p e c -> p (e c)")),
            writes=["b1t"])
        dma('gpsimd', lambda e: e.dma_start(T["b2sh"][:], d_b2sh), writes=["b2sh"])
        dma('gpsimd', lambda e: e.dma_start(T["b2r"][:], d_b2r), writes=["b2r"])
        dma('sync', lambda e: e.dma_start(T["gbias_r"][:], d_gbias), writes=["gbias_r"])

    def ffn_const_compute():
        op('vector', lambda e: e.memset(T["ones2"][:], 1.0), writes=["ones2"])
        for kc in range(DC):
            op('tensor', (lambda kc_: lambda e: e.matmul(
                P['hp0'][0:1, 0:NR], T["onesc_f"][:],
                T["cent"][:, kc_ * NR:(kc_ + 1) * NR],
                start=(kc_ == 0), stop=(kc_ == DC - 1)))(kc),
                reads=["cent", "onesc_f"], writes=[hpk(0, 0)], acc=acctag(kc, DC))
        op('vector', lambda e: e.tensor_copy(T["csum_r"][:], P['hp0'][0:1, 0:NR]),
           reads=[hpk(0, 0)], writes=["csum_r"])
        op('tensor', lambda e: e.matmul(P['yp0'][:, 0:NR], T["ones1_f"][:],
                                        T["csum_r"][:], start=True, stop=True),
           reads=["ones1_f", "csum_r"], writes=[YP0])
        op('vector', lambda e: e.tensor_copy(T["csum_b"][:], P['yp0'][:, 0:NR]),
           reads=[YP0], writes=["csum_b"])
        op('tensor', lambda e: e.matmul(P['yp1'][:, 0:NR], T["ones1_f"][:],
                                        T["gbias_r"][:], start=True, stop=True),
           reads=["ones1_f", "gbias_r"], writes=[YP1])
        op('vector', lambda e: e.tensor_copy(T["biasr_b"][:], P['yp1'][:, 0:NR]),
           reads=[YP1], writes=["biasr_b"])

    def emit_ln(with_flat):
        for i in range(DC):
            xs = f"xsq{i % 2}"
            op('scalar', (lambda i_, xs_: lambda e: e.activation(
                T[xs_][:], T[f"x{i_}"][:], AF.Square))(i, xs),
                reads=[f"x{i}"], writes=[xs])
            for half in range(2):
                sl = slice(512 * half, 512 * (half + 1))
                op('tensor', (lambda i_, sl_: lambda e: e.matmul(
                    P['hp0'][0:1, sl_], T["onesc_f"][:], T[f"x{i_}"][:, sl_],
                    start=(i_ == 0), stop=(i_ == DC - 1)))(i, sl),
                    reads=[f"x{i}", "onesc_f"], writes=[hpk(0, half)],
                    acc=acctag(i, DC))
                op('tensor', (lambda sl_, xs_, i_: lambda e: e.matmul(
                    P['hp1'][0:1, sl_], T["onesc_b"][:], T[xs_][:, sl_],
                    start=(i_ == 0), stop=(i_ == DC - 1)))(sl, xs, i),
                    reads=[xs, "onesc_b"], writes=[hpk(1, half)],
                    acc=acctag(i, DC))
        op('scalar', lambda e: e.activation(T["m_row"][:], P['hp0'][0:1, :],
                                            AF.Copy, scale=1.0 / D),
           reads=[hpk(0, 0), hpk(0, 1)], writes=["m_row"])
        op('scalar', lambda e: e.activation(T["rs_row"][:], P['hp1'][0:1, :],
                                            AF.Copy, scale=1.0 / D),
           reads=[hpk(1, 0), hpk(1, 1)], writes=["rs_row"])
        op('vector', lambda e: e.tensor_mul(T["xtmp"][0:1, :], T["m_row"][:],
                                            T["m_row"][:]),
           reads=["m_row"], writes=["xtmp"])
        op('vector', lambda e: e.tensor_sub(T["rs_row"][:], T["rs_row"][:],
                                            T["xtmp"][0:1, :]),
           reads=["rs_row", "xtmp"], writes=["rs_row"])
        op('vector', lambda e: e.tensor_scalar(T["rs_row"][:], T["rs_row"][:],
                                               EPS, None, ALU.add),
           reads=["rs_row"], writes=["rs_row"])
        op('scalar', lambda e: e.activation(T["rs_row"][:], T["rs_row"][:], AF.Ln),
           reads=["rs_row"], writes=["rs_row"])
        op('scalar', lambda e: e.activation(T["rs_row"][:], T["rs_row"][:],
                                            AF.Exp, scale=-0.5),
           reads=["rs_row"], writes=["rs_row"])
        for half in range(2):
            sl = slice(512 * half, 512 * (half + 1))
            yp = P['yp0'] if half == 0 else P['yp1']
            ypk = YP0 if half == 0 else YP1
            gpk = GPA if half == 0 else GPB
            op('tensor', (lambda sl_, yp_: lambda e: e.matmul(
                yp_[:], T["ones1_f"][:], T["m_row"][0:1, sl_],
                start=True, stop=True))(sl, yp),
                reads=["ones1_f", "m_row"], writes=[ypk])
            op('vector', (lambda sl_, yp_: lambda e: e.tensor_copy(
                T["m_b"][:, sl_], yp_[:]))(sl, yp),
                reads=[ypk], writes=["m_b"])
            op('tensor', (lambda sl_: lambda e: e.matmul(
                P['gp'][:, sl_], T["ones1_f"][:], T["rs_row"][0:1, sl_],
                start=True, stop=True))(sl),
                reads=["ones1_f", "rs_row"], writes=[gpk])
            op('vector', (lambda sl_: lambda e: e.tensor_copy(
                T["rs_b"][:, sl_], P['gp'][:, sl_]))(sl),
                reads=[gpk], writes=["rs_b"])
        for i in range(DC):
            op('vector', (lambda i_: lambda e: e.tensor_sub(
                T["xtmp"][:], T[f"x{i_}"][:], T["m_b"][:]))(i),
                reads=[f"x{i}", "m_b"], writes=["xtmp"])
            op('vector', (lambda i_: lambda e: e.tensor_mul(
                T[f"xn{i_}"][:], T["xtmp"][:], T["rs_b"][:]))(i),
                reads=["xtmp", "rs_b"], writes=[f"xn{i}"])
        del with_flat

    def emit_flat():
        for i in range(DC):
            op('vector', (lambda i_: lambda e: e.scalar_tensor_tensor(
                T[f"x{i_}"][:], T[f"xn{i_}"][:], 0.5, T[f"x{i_}"][:],
                ALU.mult, ALU.add))(i),
                reads=[f"xn{i}", f"x{i}"], writes=[f"x{i}"])

    def emit_statcols():
        for c in range(8):
            op('tensor', (lambda c_: lambda e: e.transpose(
                P['yp0'][:, 0:1], T["m_row"][0:1, 128 * c_:128 * (c_ + 1)],
                T["ident_f"][0:1, 0:1]))(c),
                reads=["m_row", "ident_f"], writes=[YP0])
            op('vector', (lambda c_: lambda e: e.tensor_copy(
                T["mcol"][:, c_:c_ + 1], P['yp0'][:, 0:1]))(c),
                reads=[YP0], writes=["mcol"])
            op('tensor', (lambda c_: lambda e: e.transpose(
                P['yp1'][:, 0:1], T["rs_row"][0:1, 128 * c_:128 * (c_ + 1)],
                T["ident_f"][0:1, 0:1]))(c),
                reads=["rs_row", "ident_f"], writes=[YP1])
            op('vector', (lambda c_: lambda e: e.tensor_copy(
                T["rscol"][:, c_:c_ + 1], P['yp1'][:, 0:1]))(c),
                reads=[YP1], writes=["rscol"])

    def emit_gating():
        emit_statcols()
        for tt in range(8):
            for kc in range(DC):
                op('tensor', (lambda tt_, kc_: lambda e: e.matmul(
                    P['yp0'][:, 0:NR], T[f"x{kc_}"][:, 128 * tt_:128 * (tt_ + 1)],
                    T["cent"][:, kc_ * NR:(kc_ + 1) * NR],
                    start=(kc_ == 0), stop=(kc_ == DC - 1)))(tt, kc),
                    reads=[f"x{kc}", "cent"], writes=[YP0], acc=acctag(kc, DC))
            op('vector', lambda e: e.tensor_copy(T["Dt"][:], P['yp0'][:, 0:NR]),
               reads=[YP0], writes=["Dt"])
            op('vector', (lambda tt_: lambda e: e.tensor_scalar(
                T["sc0"][:], T["csum_b"][:], T["mcol"][:, tt_:tt_ + 1], None,
                ALU.mult))(tt),
                reads=["csum_b", "mcol"], writes=["sc0"])
            op('vector', lambda e: e.tensor_sub(T["sc1"][:], T["Dt"][:], T["sc0"][:]),
               reads=["Dt", "sc0"], writes=["sc1"])
            op('vector', (lambda tt_: lambda e: e.tensor_scalar(
                T["sc1"][:], T["sc1"][:], T["rscol"][:, tt_:tt_ + 1], None,
                ALU.mult))(tt),
                reads=["sc1", "rscol"], writes=["sc1"])
            op('scalar', lambda e: e.activation(T["sc0"][:], T["sc1"][:], AF.Exp,
                                                scale=-1.0),
               reads=["sc1"], writes=["sc0"])
            op('vector', lambda e: e.tensor_scalar(T["sc0"][:], T["sc0"][:], 1.0,
                                                   None, ALU.add),
               reads=["sc0"], writes=["sc0"])
            op('vector', lambda e: e.reciprocal(T["sc0"][:], T["sc0"][:]),
               reads=["sc0"], writes=["sc0"])
            op('vector', lambda e: e.tensor_add(T["sc1"][:], T["sc0"][:],
                                                T["biasr_b"][:]),
               reads=["sc0", "biasr_b"], writes=["sc1"])
            op('vector', lambda e: e.tensor_copy(T["w0"][:], T["sc1"][:]),
               reads=["sc1"], writes=["w0"])
            for it in range(KR):
                op('vector', lambda e: e.reduce_max(T["smallc"][:, 0:1], T["w0"][:],
                                                    axis=AX.X),
                   reads=["w0"], writes=["smallc"])
                if it < KR - 1:
                    op('vector', lambda e: e.tensor_scalar(
                        T["w1t"][:], T["w0"][:], T["smallc"][:, 0:1], None,
                        ALU.is_ge),
                        reads=["w0", "smallc"], writes=["w1t"])
                    op('vector', lambda e: e.scalar_tensor_tensor(
                        T["w0"][:], T["w1t"][:], NEG, T["w0"][:], ALU.mult,
                        ALU.add),
                        reads=["w1t", "w0"], writes=["w0"])
            op('vector', lambda e: e.tensor_scalar(
                T["w1t"][:], T["sc1"][:], T["smallc"][:, 0:1], None, ALU.is_ge),
                reads=["sc1", "smallc"], writes=["w1t"])
            op('vector', lambda e: e.tensor_mul(T["gat"][:], T["sc1"][:],
                                                T["w1t"][:]),
               reads=["sc1", "w1t"], writes=["gat"])
            op('vector', lambda e: e.reduce_sum(T["smallc"][:, 1:2], T["gat"][:],
                                                axis=AX.X),
               reads=["gat"], writes=["smallc"])
            op('vector', lambda e: e.tensor_scalar(
                T["smallc"][:, 1:2], T["smallc"][:, 1:2], 1e-8, None, ALU.max),
                reads=["smallc"], writes=["smallc"])
            op('vector', lambda e: e.reciprocal(T["smallc"][:, 2:3],
                                                T["smallc"][:, 1:2]),
               reads=["smallc"], writes=["smallc"])
            op('vector', lambda e: e.tensor_scalar(
                T["gat"][:], T["gat"][:], T["smallc"][:, 2:3], 0.5, ALU.mult,
                ALU.mult),
                reads=["gat", "smallc"], writes=["gat"])
            op('vector', lambda e: e.tensor_copy(T["gat_b"][:], T["gat"][:]),
               reads=["gat"], writes=["gat_b"])
            op('tensor', (lambda tt_: lambda e: e.transpose(
                gp_bf[0:NR, 128 * tt_:128 * (tt_ + 1)], T["gat_b"][:],
                T["ident_bf"][:]))(tt),
                reads=["gat_b", "ident_bf"], writes=[GPA])
        op('vector', lambda e: e.tensor_copy(T["G_bf"][:], gp_bf[0:NR, 0:TOK]),
           reads=[GPA], writes=["G_bf"])
        dma('sync', lambda e: e.dma_start(d_gspill, T["G_bf"][:]),
            reads=["G_bf"], writes=["gspill"])

    def emit_experts(mid_hook=None):
        groups = [(0, 1, True)] + [(2 + 2 * g, 3 + 2 * g, False)
                                   for g in range(NGRP)]
        w1slot = [0]
        w2slot = [0]

        def load_w1(e_idx, fb):
            b = w1slot[0]
            w1slot[0] = (b + 1) % NW1
            dma('gpsimd', (lambda e_, fb_, b_: lambda e: e.dma_start(
                T[f"w1s{b_}"][:].rearrange("p (a c) -> p a c", c=128),
                d_w1[e_, fb_]))(e_idx, fb, b),
                writes=[f"w1s{b}"])
            return b

        def load_w2(e_idx, dm):
            b = w2slot[0]
            w2slot[0] = (b + 1) % NW2
            dma('gpsimd', (lambda e_, dm_, b_: lambda e: e.dma_start(
                T[f"w2s{b_}"][:].rearrange("p (a c) -> p a c", c=128),
                d_w2[e_, dm_]))(e_idx, dm, b),
                writes=[f"w2s{b}"])
            return b

        def prep_gb(e_idx, hb):
            r = e_idx - NSH
            dma('sync', (lambda r_: lambda e: e.dma_start(
                T["g_row"][:], d_gspill[r_:r_ + 1, :]))(r),
                reads=["gspill"], writes=["g_row"])
            for half in range(2):
                sl = slice(512 * half, 512 * (half + 1))
                op('tensor', (lambda sl_: lambda e: e.matmul(
                    P['gp'][:, sl_], T["ones1_b"][:], T["g_row"][0:1, sl_],
                    start=True, stop=True))(sl),
                    reads=["ones1_b", "g_row"],
                    writes=[GPA if half == 0 else GPB])
            op('vector', (lambda hb_: lambda e: e.tensor_copy(
                T[f"gb{hb_}"][:], P['gp'][:]))(hb),
                reads=[GPA, GPB], writes=[f"gb{hb}"])

        def emit_h(e_idx, shared):
            hb = e_idx % 2
            if not shared:
                prep_gb(e_idx, hb)
            for fb in range(FC):
                wb = load_w1(e_idx, fb)
                pb = fb % 2
                hp = P[f'hp{pb}']
                for kc in range(DC):
                    for half in range(2):
                        sl = slice(512 * half, 512 * (half + 1))
                        op('tensor', (lambda wb_, kc_, sl_, hp_, i_: lambda e: e.matmul(
                            hp_[:, sl_],
                            T[f"w1s{wb_}"][:, 128 * kc_:128 * (kc_ + 1)],
                            T[f"xn{kc_}"][:, sl_],
                            start=(i_ == 0), stop=(i_ == DC - 1)))(wb, kc, sl, hp, kc),
                            reads=[f"w1s{wb}", f"xn{kc}"], writes=[hpk(pb, half)],
                            acc=acctag(kc, DC))
                for half in range(2):
                    sl = slice(512 * half, 512 * (half + 1))
                    op('scalar', (lambda e_, fb_, sl_, hb_, hp_: lambda e: e.activation(
                        T[f"h{hb_}_{fb_}"][:, sl_], hp_[:, sl_], AF.Relu,
                        bias=T["b1t"][:, e_ * FC + fb_:e_ * FC + fb_ + 1]))(
                            e_idx, fb, sl, hb, P[f'hp{pb}']),
                        reads=[hpk(pb, half), "b1t"], writes=[f"h{hb}_{fb}"])
                if not shared:
                    op('vector', (lambda fb_, hb_: lambda e: e.tensor_mul(
                        T[f"h{hb_}_{fb_}"][:], T[f"h{hb_}_{fb_}"][:],
                        T[f"gb{hb_}"][:]))(fb, hb),
                        reads=[f"h{hb}_{fb}", f"gb{hb}"], writes=[f"h{hb}_{fb}"])

        def emit_y_group(e0, e1, shared, first_routed):
            for dm in range(DC):
                wb0 = load_w2(e0, dm)
                wb1 = load_w2(e1, dm)
                yp = P['yp0'] if dm % 2 == 0 else P['yp1']
                ypk = YP0 if dm % 2 == 0 else YP1
                for half in range(2):
                    sl = slice(512 * half, 512 * (half + 1))
                    mms = [(e0, wb0, fb) for fb in range(FC)] + \
                          [(e1, wb1, fb) for fb in range(FC)]
                    extra = 1 if (shared or first_routed) else 0
                    total = len(mms) + extra
                    for idx, (ee, wb, fb) in enumerate(mms):
                        op('tensor', (lambda wb_, fb_, ee_, sl_, yp_, i_, n_:
                                      lambda e: e.matmul(
                            yp_[:], T[f"w2s{wb_}"][:, 128 * fb_:128 * (fb_ + 1)],
                            T[f"h{ee_ % 2}_{fb_}"][:, sl_],
                            start=(i_ == 0), stop=(i_ == n_ - 1)))(
                                wb, fb, ee, sl, yp, idx, total),
                            reads=[f"w2s{wb}", f"h{ee % 2}_{fb}"], writes=[ypk],
                            acc=acctag(idx, total))
                    if shared:
                        op('tensor', (lambda dm_, sl_, yp_: lambda e: e.matmul(
                            yp_[:], T["b2sh"][:, 128 * dm_:128 * (dm_ + 1)],
                            T["ones2"][:, sl_], start=False, stop=True))(dm, sl, yp),
                            reads=["b2sh", "ones2"], writes=[ypk], acc='stop')
                    elif first_routed:
                        op('tensor', (lambda dm_, sl_, yp_: lambda e: e.matmul(
                            yp_[:], T["b2r"][:, 128 * dm_:128 * (dm_ + 1)],
                            T["G_bf"][:, sl_], start=False, stop=True))(dm, sl, yp),
                            reads=["b2r", "G_bf"], writes=[ypk], acc='stop')
                    if shared:
                        op('vector', (lambda dm_, sl_, yp_: lambda e:
                                      e.scalar_tensor_tensor(
                            T[f"x{dm_}"][:, sl_], yp_[:], 0.5,
                            T[f"x{dm_}"][:, sl_], ALU.mult, ALU.add))(dm, sl, yp),
                            reads=[ypk, f"x{dm}"], writes=[f"x{dm}"])
                    else:
                        op('vector', (lambda dm_, sl_, yp_: lambda e: e.tensor_add(
                            T[f"x{dm_}"][:, sl_], T[f"x{dm_}"][:, sl_], yp_[:]))(
                                dm, sl, yp),
                            reads=[ypk, f"x{dm}"], writes=[f"x{dm}"])
        emit_h(0, True)
        emit_h(1, True)
        if mid_hook is not None:
            mid_hook()
        emit_y_group(0, 1, True, False)
        for gi, (e0, e1, shared) in enumerate(groups[1:]):
            emit_h(e0, shared)
            emit_h(e1, shared)
            emit_y_group(e0, e1, False, gi == 0)

    def emit_ffn(skip_const_dmas=False):
        if not skip_const_dmas:
            ffn_const_dmas()
        ffn_const_compute()
        emit_ln(with_flat=True)
        emit_experts(mid_hook=lambda: (emit_gating(), emit_flat()))

    def emit_qkv(order):
        wslot = [0]
        for nb in order:
            b = wslot[0]
            wslot[0] = 1 - b
            dma('gpsimd', (lambda nb_, b_: lambda e: e.dma_start(
                T[f"w1s{b_}"][:].rearrange("p (a c) -> p a c", c=128),
                d_wqkv[nb_]))(nb, b),
                writes=[f"w1s{b}"])
            pb = nb % 2
            hp = P[f'hp{pb}']
            for kc in range(DC):
                for half in range(2):
                    sl = slice(512 * half, 512 * (half + 1))
                    op('tensor', (lambda b_, kc_, sl_, hp_: lambda e: e.matmul(
                        hp_[:, sl_], T[f"w1s{b_}"][:, 128 * kc_:128 * (kc_ + 1)],
                        T[f"xn{kc_}"][:, sl_],
                        start=(kc_ == 0), stop=(kc_ == DC - 1)))(b, kc, sl, hp),
                        reads=[f"w1s{b}", f"xn{kc}"], writes=[hpk(pb, half)],
                        acc=acctag(kc, DC))
            if nb < 24:
                st = f"qst{nb % 2}"
                op('scalar', (lambda st_, hp_: lambda e: e.activation(
                    T[st_][:], hp_[:], AF.Copy, scale=SCALE))(st, P[f'hp{pb}']),
                    reads=[hpk(pb, 0), hpk(pb, 1)], writes=[st])
                dma('sync', (lambda nb_, st_: lambda e: e.dma_start(
                    d_qspill[128 * nb_:128 * (nb_ + 1), :], T[st_][:]))(nb, st),
                    reads=[st], writes=[f"qsp{nb}"])
            else:
                c = nb - 24
                op('scalar', (lambda c_, hp_: lambda e: e.activation(
                    T[f"ckvl{c_}"][:], hp_[:], AF.Copy))(c, P[f'hp{pb}']),
                    reads=[hpk(pb, 0), hpk(pb, 1)], writes=[f"ckvl{c}"])
                dma('sync', (lambda c_: lambda e: e.dma_start(
                    d_cin[128 * c_:128 * (c_ + 1), :], T[f"ckvl{c_}"][:]))(c),
                    reads=[f"ckvl{c}"], writes=["cin"])

    def emit_collective():
        if dbg.get("mock_collective"):
            for r in range(4):
                dma('gpsimd', (lambda r_: lambda e: e.dma_start(
                    d_cout[LORA * r_:LORA * (r_ + 1), :], d_cin))(r),
                    reads=["cin"], writes=["cout"])
        else:
            s.op('gpsimd', lambda e: e.collective_compute(
                "AllGather", ALU.bypass,
                replica_groups=[[0, 1, 2, 3], [4, 5, 6, 7]],
                ins=[d_cin], outs=[d_cout]),
                reads=["cin"], writes=["cout"], coll=True)

    def emit_kv(kvh, wb):
        dma('gpsimd', (lambda kvh_, wb_: lambda e: e.dma_start(
            T[f"wks{wb_}"][:].rearrange("p (a c) -> p a c", c=192),
            d_wk[kvh_]))(kvh, wb),
            writes=[f"wks{wb}"])
        dma('gpsimd', (lambda kvh_, wb_: lambda e: e.dma_start(
            T[f"wvs{wb_}"][:].rearrange("p (a c) -> p a c", c=128),
            d_wv[kvh_]))(kvh, wb),
            writes=[f"wvs{wb}"])
        cout_r = d_cout.rearrange("(j c2 p) t -> p j c2 t", p=128, j=4)
        for nt in range(8):
            cb = nt % 2
            cs = f"cs{cb}"
            for kb in range(4):
                dst = T[cs][:, 512 * kb:512 * (kb + 1)].rearrange(
                    "p (j o) -> p j o", o=128)
                srcap = cout_r[:, :, kb, 128 * nt:128 * (nt + 1)]
                dma('sync', (lambda d2, s2: lambda e: e.dma_start(d2, s2))(dst, srcap),
                    reads=["cout"], writes=[cs])
            sl = slice(512 * nt, 512 * (nt + 1))
            yp = P['yp0'] if nt % 2 == 0 else P['yp1']
            ypk = YP0 if nt % 2 == 0 else YP1
            for kb in range(4):
                op('tensor', (lambda wb_, kb_, yp_, cs_: lambda e: e.matmul(
                    yp_[:], T[f"wks{wb_}"][:, 192 * kb_:192 * kb_ + 128],
                    T[cs_][:, 512 * kb_:512 * (kb_ + 1)],
                    start=(kb_ == 0), stop=(kb_ == 3)))(wb, kb, yp, cs),
                    reads=[f"wks{wb}", cs], writes=[ypk], acc=acctag(kb, 4))
            op('scalar', (lambda sl_, yp_: lambda e: e.activation(
                T["kn"][:, sl_], yp_[:], AF.Copy))(sl, yp),
                reads=[ypk], writes=["kn"])
            for kb in range(4):
                op('tensor', (lambda wb_, kb_, yp_, cs_: lambda e: e.matmul(
                    yp_[0:64, :],
                    T[f"wks{wb_}"][:, 192 * kb_ + 128:192 * (kb_ + 1)],
                    T[cs_][:, 512 * kb_:512 * (kb_ + 1)],
                    start=(kb_ == 0), stop=(kb_ == 3)))(wb, kb, yp, cs),
                    reads=[f"wks{wb}", cs], writes=[ypk], acc=acctag(kb, 4))
            op('scalar', (lambda sl_, yp_: lambda e: e.activation(
                T["krr"][0:64, sl_], yp_[0:64, :], AF.Copy))(sl, yp),
                reads=[ypk], writes=["krr"])
            for tc in range(4):
                gh = P['gp'][:, 0:128] if tc % 2 == 0 else P['gp'][:, 512:640]
                gk = GPA if tc % 2 == 0 else GPB
                for kb in range(4):
                    op('tensor', (lambda wb_, kb_, tc_, gh_, cs_: lambda e: e.matmul(
                        gh_, T[cs_][:, 512 * kb_ + 128 * tc_:512 * kb_ + 128 * (tc_ + 1)],
                        T[f"wvs{wb_}"][:, 128 * kb_:128 * (kb_ + 1)],
                        start=(kb_ == 0), stop=(kb_ == 3)))(wb, kb, tc, gh, cs),
                        reads=[f"wvs{wb}", cs], writes=[gk], acc=acctag(kb, 4))
                op('scalar', (lambda nt_, tc_, gh_: lambda e: e.activation(
                    T["Vh"][:, 128 * (4 * nt_ + tc_):128 * (4 * nt_ + tc_ + 1)],
                    gh_, AF.Copy))(nt, tc, gh),
                    reads=[gk], writes=["Vh"])
        dma('sync', lambda e: e.dma_start(d_kswap, T["krr"][0:64, :]),
            reads=["krr"], writes=["kswap"])
        dma('sync', lambda e: e.dma_start(T["ks"][0:32, :], d_kswap[32:64, :]),
            reads=["kswap"], writes=["ks"])
        dma('sync', lambda e: e.dma_start(T["ks"][32:64, :], d_kswap[0:32, :]),
            reads=["kswap"], writes=["ks"])
        op('vector', lambda e: e.tensor_mul(T["krr"][0:64, :], T["krr"][0:64, :],
                                            T["kcos"][:]),
           reads=["krr", "kcos"], writes=["krr"])
        op('vector', lambda e: e.tensor_mul(T["ks"][:], T["ks"][:], T["ksin"][:]),
           reads=["ks", "ksin"], writes=["ks"])
        op('vector', lambda e: e.tensor_add(T["krr"][0:64, :], T["krr"][0:64, :],
                                            T["ks"][:]),
           reads=["krr", "ks"], writes=["krr"])
        op('vector', lambda e: e.memset(T["krr"][64:65, :], 1.0), writes=["krr"])

    def emit_qload(h, slot):
        lo, hi = 192 * h, 192 * h + 192
        qsp_reads = [f"qsp{nb}" for nb in range(lo // 128, (hi - 1) // 128 + 1)]
        dma('gpsimd', (lambda h_, s_: lambda e: e.dma_start(
            T[f"qn{s_}"][:], d_qspill[192 * h_:192 * h_ + 128, :]))(h, slot),
            reads=qsp_reads, writes=[f"qn{slot}"])
        dma('gpsimd', (lambda h_, s_: lambda e: e.dma_start(
            T[f"qr{s_}"][0:64, :], d_qspill[192 * h_ + 128:192 * h_ + 192, :]))(h, slot),
            reads=qsp_reads, writes=[f"qr{slot}"])
        dma('gpsimd', (lambda h_: lambda e: e.dma_start(
            T["qs"][0:32, :], d_qspill[192 * h_ + 160:192 * h_ + 192, :]))(h),
            reads=qsp_reads, writes=["qs"])
        dma('gpsimd', (lambda h_: lambda e: e.dma_start(
            T["qs"][32:64, :], d_qspill[192 * h_ + 128:192 * h_ + 160, :]))(h),
            reads=qsp_reads, writes=["qs"])
        op('vector', (lambda s_: lambda e: e.tensor_mul(
            T[f"qr{s_}"][0:64, :], T[f"qr{s_}"][0:64, :], T["qcos"][:]))(slot),
            reads=[f"qr{slot}", "qcos"], writes=[f"qr{slot}"])
        op('vector', lambda e: e.tensor_mul(T["qs"][:], T["qs"][:], T["qsin"][:]),
           reads=["qs", "qsin"], writes=["qs"])
        op('vector', (lambda s_: lambda e: e.tensor_add(
            T[f"qr{s_}"][0:64, :], T[f"qr{s_}"][0:64, :], T["qs"][:]))(slot),
            reads=[f"qr{slot}", "qs"], writes=[f"qr{slot}"])

    def p1_gen(h, slot):
        nmx = f"nmx{slot}"
        rcp = f"rcp{slot}"
        """Pass 1 of one head: scores S [q x k], per-row max (bf16-rounded),
        sum of exp (accum_out). Yields once per sub so the driver can
        interleave pass-2 tiles of the previous head."""
        qn, qr = f"qn{slot}", f"qr{slot}"
        sp_banks = [(P['hp0'], slice(0, 512), hpk(0, 0)),
                    (P['hp0'], slice(512, 1024), hpk(0, 1)),
                    (P['hp1'], slice(0, 512), hpk(1, 0)),
                    (P['hp1'], slice(512, 1024), hpk(1, 1))]
        for k in range(8):
            Lk = 512 * (k + 1)
            Sb = f"S{k % 2}"
            sx = f"sx{k % 2}"
            bm = f"bm{k % 2}"
            qsl = slice(128 * k, 128 * (k + 1))
            for blk in range(k + 1):
                hp_, psl, key = sp_banks[blk % 4]
                ksl = slice(512 * blk, 512 * (blk + 1))
                op('tensor', (lambda hp2, psl2, qsl2, ksl2: lambda e: e.matmul(
                    hp2[:, psl2], T[qn][:, qsl2], T["kn"][:, ksl2],
                    start=True, stop=False))(hp_, psl, qsl, ksl),
                    reads=[qn, "kn"], writes=[key], acc='start')
                op('tensor', (lambda hp2, psl2, qsl2, ksl2: lambda e: e.matmul(
                    hp2[:, psl2], T[qr][0:64, qsl2], T["krr"][0:64, ksl2],
                    start=False, stop=True))(hp_, psl, qsl, ksl),
                    reads=[qr, "krr"], writes=[key], acc='stop')
                if blk < k:
                    op('scalar', (lambda hp2, psl2, ksl2, Sb_: lambda e:
                                  e.activation(T[Sb_][:, ksl2], hp2[:, psl2],
                                               AF.Copy))(hp_, psl, ksl, Sb),
                        reads=[key], writes=[Sb])
                else:
                    op('vector', (lambda hp2, psl2, ksl2, Sb_: lambda e: e.tensor_add(
                        T[Sb_][:, ksl2], hp2[:, psl2], T["maskf"][:]))(
                            hp_, psl, ksl, Sb),
                        reads=[key, "maskf"], writes=[Sb])
                op('vector', (lambda b_, ksl2, Sb_, bm_: lambda e: e.reduce_max(
                    T[bm_][:, b_:b_ + 1], T[Sb_][:, ksl2], axis=AX.X))(
                        blk, ksl, Sb, bm),
                    reads=[Sb], writes=[bm])
            op('vector', (lambda k_, bm_, sx_: lambda e: e.reduce_max(
                T[sx_][:, 3:4], T[bm_][:, 0:k_ + 1], axis=AX.X, negate=True))(
                    k, bm, sx),
                reads=[bm], writes=[sx])
            op('vector', (lambda sx_, k_, nm_: lambda e: e.tensor_copy(
                T[nm_][:, k_:k_ + 1], T[sx_][:, 3:4]))(sx, k, nmx),
                reads=[sx], writes=[nmx])
            yield

    def p2_gen(h, slot):
        nmx = f"nmx{slot}"
        rcp = f"rcp{slot}"
        """Pass 2 of one head: S^T tiles (with -max as an augmented contraction
        row), Exp on evacuation into P^T, AV accumulation, 1/sum on the output.
        Yields between tiles."""
        qn, qr = f"qn{slot}", f"qr{slot}"
        hp_bf0 = P['hp0'][:].bitcast(BF16)
        for g in range(2):
            Lg = 512 * (4 * g + 4)
            nkc = Lg // 128
            rowyp = P['yp1'] if g % 2 == 0 else P['yp0']
            rowk = YP1 if g % 2 == 0 else YP0
            rowbf = rowyp[:].bitcast(BF16)
            for i in range(4):
                k = 4 * g + i
                op('tensor', (lambda k_, i_, rb_, nm_: lambda e: e.transpose(
                    rb_[0:1, 128 * i_:128 * (i_ + 1)],
                    T[nm_][:, k_:k_ + 1], T["ident_bf"][:]))(k, i, rowbf, nmx),
                    reads=[nmx, "ident_bf"], writes=[rowk])
            op('vector', (lambda rb_: lambda e: e.tensor_copy(
                T["nmrow"][:], rb_[0:1, 0:512]))(rowbf),
                reads=[rowk], writes=["nmrow"])
            dma('sync', (lambda g_, qr_: lambda e: e.dma_start(
                T[qr_][64:65, 512 * g_:512 * (g_ + 1)], T["nmrow"][:]))(g, qr),
                reads=["nmrow"], writes=[qr])
            yield
            yp = P['yp0'] if g % 2 == 0 else P['yp1']
            ypk = YP0 if g % 2 == 0 else YP1
            LA = 5
            pp_rot = [(P['gp'][:, 0:512], GPA), (P['gp'][:, 512:1024], GPB)]
            for it in range(nkc + LA):
                if it < nkc:
                    kc = it
                    slot2 = kc % NPT
                    pp, ppk = pp_rot[kc % 2]
                    kdiag = kc // 4
                    cst = 0 if kdiag < 4 * g else 128 * (kdiag - 4 * g)
                    stair = kdiag >= 4 * g
                    csl = slice(cst, 512)
                    qsl2 = slice(512 * g + cst, 512 * (g + 1))
                    kcs = slice(128 * kc, 128 * (kc + 1))
                    op('tensor', (lambda pp_, csl_, qsl_, kcs_: lambda e: e.matmul(
                        pp_[:, csl_], T["kn"][:, kcs_], T[qn][:, qsl_],
                        start=True, stop=False))(pp, csl, qsl2, kcs),
                        reads=["kn", qn], writes=[ppk], acc='start')
                    op('tensor', (lambda pp_, csl_, qsl_, kcs_: lambda e: e.matmul(
                        pp_[:, csl_], T["krr"][:, kcs_], T[qr][:, qsl_],
                        start=False, stop=True))(pp, csl, qsl2, kcs),
                        reads=["krr", qr], writes=[ppk], acc='stop')
                    if stair:
                        r = kc % 4
                        op('vector', (lambda pp_, cst_, r_: lambda e: e.tensor_add(
                            pp_[:, cst_:cst_ + 128], pp_[:, cst_:cst_ + 128],
                            T["maskt"][:, 128 * r_:128 * (r_ + 1)]))(pp, cst, r),
                            reads=[ppk, "maskt"], writes=[ppk])
                    op('scalar', (lambda pp_, csl_, sl_: lambda e: e.activation(
                        T[f"pt{sl_}"][:, csl_], pp_[:, csl_], AF.Exp))(
                            pp, csl, slot2),
                        reads=[ppk], writes=[f"pt{slot2}"])
                    op('tensor', (lambda ry_, csl_, sl_, kc_, n_: lambda e: e.matmul(
                        ry_[0:1, csl_], T["onesc_b"][:], T[f"pt{sl_}"][:, csl_],
                        start=(kc_ == 0), stop=(kc_ == n_ - 1)))(
                            rowyp, csl, slot2, kc, nkc),
                        reads=["onesc_b", f"pt{slot2}"], writes=[rowk],
                        acc=acctag(kc, nkc))
                if it >= LA:
                    kc = it - LA
                    slot2 = kc % NPT
                    kdiag2 = kc // 4
                    cst2 = 0 if kdiag2 < 4 * g else 128 * (kdiag2 - 4 * g)
                    csl2 = slice(cst2, 512)
                    op('tensor', (lambda kc_, sl_, yp_, n_, c2: lambda e: e.matmul(
                        yp_[:, c2], T["Vh"][:, 128 * kc_:128 * (kc_ + 1)],
                        T[f"pt{sl_}"][:, c2],
                        start=(kc_ == 0), stop=(kc_ == n_ - 1)))(
                            kc, slot2, yp, nkc, csl2),
                        reads=["Vh", f"pt{slot2}"], writes=[ypk],
                        acc=acctag(kc, nkc))
                yield
            op('vector', (lambda ry_: lambda e: e.tensor_copy(
                T["srow"][:], ry_[0:1, 0:512]))(rowyp),
                reads=[rowk], writes=["srow"])
            op('vector', lambda e: e.reciprocal(T["rcrow"][:], T["srow"][:]),
               reads=["srow"], writes=["rcrow"])
            op('tensor', (lambda ry_: lambda e: e.matmul(
                ry_[:], T["ones1_f"][:], T["rcrow"][:],
                start=True, stop=True))(rowyp),
                reads=["ones1_f", "rcrow"], writes=[rowk])
            op('vector', (lambda ry_: lambda e: e.tensor_copy(
                T["rcb"][:], ry_[:]))(rowyp),
                reads=[rowk], writes=["rcb"])
            ob = g % 2
            op('vector', (lambda ob_, yp_: lambda e: e.tensor_tensor(
                T[f"ost{ob_}"][:], yp_[:], T["rcb"][:], ALU.mult))(ob, yp),
                reads=[ypk, "rcb"], writes=[f"ost{ob}"])
            dma('sync', (lambda h_, g_, ob_: lambda e: e.dma_start(
                d_ospill[128 * h_:128 * (h_ + 1), 512 * g_:512 * (g_ + 1)],
                T[f"ost{ob_}"][:]))(h, g, ob),
                reads=[f"ost{ob}"], writes=[f"osp{h}"])
            yield

    def drain(gen):
        if gen is not None:
            for _ in gen:
                pass

    def emit_attention():
        emit_ln(with_flat=False)
        emit_qkv([24, 25, 26, 27])
        emit_collective()
        emit_qkv(list(range(24)))
        s.barrier()
        dma('gpsimd', lambda e: e.dma_start(T["kcos"][:], d_kcos), writes=["kcos"])
        dma('gpsimd', lambda e: e.dma_start(T["ksin"][:], d_ksin), writes=["ksin"])
        dma('gpsimd', lambda e: e.dma_start(T["qcos"][:], d_qcos), writes=["qcos"])
        dma('gpsimd', lambda e: e.dma_start(T["qsin"][:], d_qsin), writes=["qsin"])
        dma('sync', lambda e: e.dma_start(T["maskf"][:], d_mask), writes=["maskf"])
        dma('sync', lambda e: e.dma_start(
            T["maskt"][:].rearrange("p (r o) -> p r o", o=128),
            d_maskt.rearrange("r p o -> p r o")), writes=["maskt"])
        pending = None
        for i, kvh in enumerate(dbg.get("kvh_list", (0, 1, 2, 3))):
            drain(pending)
            pending = None
            emit_kv(kvh, i % 2)
            for qi in range(4):
                h = 4 * kvh + qi
                if qi == 0:
                    emit_qload(h, h % 2)
                p1 = p1_gen(h, h % 2)
                psteps = list(pending) if False else pending
                nsub = 8
                for si_ in range(nsub):
                    if next(p1, 'done') == 'done':
                        break
                    if pending is not None:
                        budget = 8 if si_ < nsub - 1 else 10 ** 9
                        for _ in range(budget):
                            if next(pending, 'done') == 'done':
                                pending = None
                                break
                for _ in p1:
                    pass
                drain(pending)
                pending = p2_gen(h, h % 2)
                if qi < 3:
                    emit_qload(4 * kvh + qi + 1, (h + 1) % 2)
        drain(pending)
        s.barrier()
        if dbg.get("do_ffn2", True):
            ffn_const_dmas()
        computed = {4 * kvh + qi for kvh in dbg.get("kvh_list", (0, 1, 2, 3))
                    for qi in range(4)}
        for kb in range(16):
            if kb in computed:
                dma('sync', (lambda kb_: lambda e: e.dma_start(
                    T[f"Ores{kb_}"][:], d_ospill[128 * kb_:128 * (kb_ + 1), :]))(kb),
                    reads=[f"osp{kb}"], writes=[f"Ores{kb}"])
            else:
                op('vector', (lambda kb_: lambda e: e.memset(
                    T[f"Ores{kb_}"][:], 0.0))(kb), writes=[f"Ores{kb}"])
        woslot = [0]
        for dm in range(DC):
            b = woslot[0]
            woslot[0] = 1 - b
            dma('gpsimd', (lambda dm_, b_: lambda e: e.dma_start(
                T[f"wos{b_}"][:].rearrange("p (a c) -> p a c", c=128),
                d_wo[dm_]))(dm, b),
                writes=[f"wos{b}"])
            pb = dm % 2
            hp = P[f'hp{pb}']
            for kb in range(16):
                for half in range(2):
                    sl = slice(512 * half, 512 * (half + 1))
                    op('tensor', (lambda b_, kb_, sl_, hp_: lambda e: e.matmul(
                        hp_[:, sl_], T[f"wos{b_}"][:, 128 * kb_:128 * (kb_ + 1)],
                        T[f"Ores{kb_}"][:, sl_],
                        start=(kb_ == 0), stop=(kb_ == 15)))(b, kb, sl, hp),
                        reads=[f"wos{b}", f"Ores{kb}"], writes=[hpk(pb, half)],
                        acc=acctag(kb, 16))
            for half in range(2):
                sl = slice(512 * half, 512 * (half + 1))
                op('vector', (lambda dm_, sl_, hp_: lambda e: e.tensor_add(
                    T[f"x{dm_}"][:, sl_], T[f"x{dm_}"][:, sl_], hp_[:, sl_]))(
                        dm, sl, P[f'hp{pb}']),
                    reads=[hpk(pb, half), f"x{dm}"], writes=[f"x{dm}"])

    emit_ffn()
    s.barrier()
    if dbg.get("do_attn", True):
        emit_attention()
        s.barrier()
    if dbg.get("do_ffn2", True):
        emit_ffn(skip_const_dmas=dbg.get("do_attn", True))
        s.barrier()
    for i in range(DC):
        dma('sync', (lambda i_: lambda e: e.dma_start(
            d_out[128 * i_:128 * (i_ + 1), :], T[f"x{i_}"][:]))(i),
            reads=[f"x{i}"], writes=[f"out{i}"])
    s.finalize()

    with nc.Block() as block:
        s.run(block)
    return nc


def _tile_lhsT(W, nb):
    K, N = W.shape
    return np.ascontiguousarray(
        W.reshape(K // 128, 128, N // nb, nb).transpose(2, 1, 0, 3))


def _rope_tables(pos):
    inv = (1.0 / (10000.0 ** (np.arange(0, D_ROPE, 2, dtype=np.float32)
                              / D_ROPE))).astype(np.float32)
    f = np.outer(pos.astype(np.float32), inv)
    emb = np.concatenate([f, f], axis=1)
    cos = np.cos(emb).T.astype(np.float32)
    sin = np.sin(emb).T.astype(np.float32)
    sinm = np.concatenate([-sin[0:32], sin[32:64]], axis=0)
    return np.ascontiguousarray(cos), np.ascontiguousarray(sinm)


def prep_inputs(inputs):
    wqkv = np.asarray(inputs["L0_attn_Wqkv"], np.float32)
    wkup = np.asarray(inputs["L0_attn_Wk_up"], np.float32)
    wvup = np.asarray(inputs["L0_attn_Wv_up"], np.float32)
    wo = np.asarray(inputs["L0_attn_Wo"], np.float32)
    sw1 = np.asarray(inputs["L0_ffn_shared_w1"], np.float32)
    sb1 = np.asarray(inputs["L0_ffn_shared_b1"], np.float32)
    sw2 = np.asarray(inputs["L0_ffn_shared_w2"], np.float32)
    sb2 = np.asarray(inputs["L0_ffn_shared_b2"], np.float32)
    rw1 = np.asarray(inputs["L0_ffn_routed_w1"], np.float32)
    rb1 = np.asarray(inputs["L0_ffn_routed_b1"], np.float32)
    rw2 = np.asarray(inputs["L0_ffn_routed_w2"], np.float32)
    rb2 = np.asarray(inputs["L0_ffn_routed_b2"], np.float32)
    cent = np.asarray(inputs["L0_ffn_centroids"], np.float32)
    gbias = np.asarray(inputs["L0_ffn_bias"], np.float32)
    x = np.asarray(inputs["L0_attn_hidden_states"], np.float32)

    shared = {}
    shared["wqkv"] = _tile_lhsT(wqkv, 128)
    shared["wk"] = np.ascontiguousarray(np.stack([
        wkup[:, 192 * h:192 * (h + 1)].reshape(4, 128, 192).transpose(1, 0, 2)
        for h in range(NKV)]))
    shared["wv"] = np.ascontiguousarray(np.stack([
        wvup[:, 128 * h:128 * (h + 1)].reshape(4, 128, 128).transpose(1, 0, 2)
        for h in range(NKV)]))
    shared["wo"] = _tile_lhsT(wo, 128)
    w1_all = np.concatenate([sw1, rw1], axis=0)
    w2_all = np.concatenate([sw2, rw2], axis=0)
    shared["w1"] = np.ascontiguousarray(
        np.stack([_tile_lhsT(w1_all[e], 128) for e in range(NE)]))
    shared["w2"] = np.ascontiguousarray(
        np.stack([_tile_lhsT(w2_all[e], 128) for e in range(NE)]))
    b1_all = np.concatenate([sb1, rb1], axis=0)
    shared["b1"] = np.ascontiguousarray(
        b1_all.reshape(NE, FC, 128).transpose(2, 0, 1))
    shared["b2sh"] = np.ascontiguousarray(sb2)
    shared["b2r"] = np.ascontiguousarray(rb2)
    shared["cent"] = np.ascontiguousarray(
        cent.T.reshape(DC, 128, NR).transpose(1, 0, 2))
    shared["gbias"] = np.ascontiguousarray(gbias[None, :])
    shared["ident"] = np.eye(128, dtype=np.float32)
    kcos, ksin = _rope_tables(np.arange(S))
    shared["kcos"], shared["ksin"] = kcos, ksin

    xr = x.reshape(B, 32, 128, D)
    in_maps = []
    for c in range(8):
        b, j = c // 4, c % 4
        tiles = [j + 4 * k for k in range(8)]
        xc = xr[b, tiles].reshape(TOK, D)
        m = dict(shared)
        m["x"] = np.ascontiguousarray(xc.T)
        pos = np.concatenate([np.arange(128 * t, 128 * (t + 1)) for t in tiles])
        qcos, qsin = _rope_tables(pos)
        m["qcos"], m["qsin"] = qcos, qsin
        mask = np.full((128, 512), NEG, np.float32)
        ii = np.arange(128)[:, None]
        jj = np.arange(512)[None, :]
        mask[jj <= 128 * j + ii] = 0.0
        m["mask"] = mask
        maskt = np.zeros((4, 128, 128), np.float32)
        for r in range(4):
            kk = 128 * r + np.arange(128)[:, None]
            qq = 128 * j + np.arange(128)[None, :]
            maskt[r][kk > qq] = NEG
        m["maskt"] = maskt
        in_maps.append(m)
    return in_maps


def kernel(**inputs):
    in_maps = prep_inputs(inputs)
    nc = build_nc(DEBUG)
    res = run_bass_kernel_spmd(nc, in_maps, core_ids=list(range(8)))
    out = np.zeros((B, 32, 128, D), np.float32)
    for c in range(8):
        b, j = c // 4, c % 4
        tiles = [j + 4 * k for k in range(8)]
        oc = np.asarray(res.results[c]["out"], np.float32).T.reshape(8, 128, D)
        for ki, t in enumerate(tiles):
            out[b, t] = oc[ki]
    return out.reshape(B, S, D)


if __name__ == "__main__":
    print("building program ...")
    nc = build_nc(DEBUG)
    print("instructions:", nc.n_instructions())


# version 16 (final)
# speedup vs baseline: 1.1173x; 1.1173x over previous
import numpy as np

import concourse.bass as bass
import concourse.mybir as mybir
from concourse.bass_utils import run_bass_kernel_spmd

F32 = mybir.dt.float32
BF16 = mybir.dt.bfloat16
AF = mybir.ActivationFunctionType
ALU = mybir.AluOpType
AX = mybir.AxisListType

B, S, D = 2, 4096, 2048
TOK = 1024
DC = D // 128
DFF = 1408
FC = DFF // 128
NSH, NR, NE = 2, 16, 18
KR = 4
NH, NKV = 16, 4
D_NOPE, D_ROPE, D_QK, D_V = 128, 64, 192, 128
LORA = 512
QH = NH * D_QK
SCALE = float(1.0 / np.sqrt(D_QK))
EPS = 1e-5
NEG = -1.0e30

DEBUG = dict(n_groups=8, do_attn=True, do_ffn2=True, kvh_list=(0, 1, 2, 3),
             mock_collective=False)


class Sched:
    COMPUTE = ("tensor", "scalar", "vector", "gpsimd")
    QUEUES = ("sync", "gpsimd", "scalar")
    NDMA = 12

    def __init__(self, nc):
        self.nc = nc
        self.q = {e: [] for e in ("tensor", "scalar", "vector", "gpsimd", "sync")}
        self.prog_sem = {e: nc.alloc_semaphore(f"pg_{e}") for e in self.COMPUTE}
        self.prog_cnt = {e: 0 for e in self.COMPUTE}
        self.dma_sem = {}
        self.dma_cnt = {}
        for qn in self.QUEUES:
            self.dma_sem[qn] = [nc.alloc_semaphore(f"dm_{qn}{i}")
                                for i in range(self.NDMA)]
            self.dma_cnt[qn] = 0
        self.cc_sem = nc.alloc_semaphore("cc")
        self.cc_cnt = 0
        self.res_w = {}
        self.res_r = {}
        self.sat = {e: {} for e in self.q}

    def _tok_wait(self, tok):
        if tok[0] == 'p':
            return (self.prog_sem[tok[1]], tok[2])
        if tok[0] == 'c':
            return (self.cc_sem, tok[1])
        qn, idx = tok[1], tok[2]
        return (self.dma_sem[qn][idx % self.NDMA], 16 * (idx // self.NDMA + 1))

    def _need(self, eng, tok):
        sem, val = self._tok_wait(tok)
        key = sem.num
        if val <= self.sat[eng].get(key, -1):
            return None
        self.sat[eng][key] = val
        return (sem, val)

    def op(self, eng, fn, reads=(), writes=(), dma=False, acc=None, coll=False):
        deps = []
        if dma and self.dma_cnt[eng] >= self.NDMA:
            deps.append(('d', eng, self.dma_cnt[eng] - self.NDMA))
        for r in reads:
            t = self.res_w.get(r)
            if t is not None:
                deps.append(t)
        if acc in (None, 'start'):
            for w in writes:
                t = self.res_w.get(w)
                if t is not None:
                    deps.append(t)
                deps.extend(self.res_r.get(w, ()))
        waits = []
        for t in deps:
            wv = self._need(eng, t)
            if wv:
                waits.append(wv)
        if coll:
            self.cc_cnt += 1
            tok = ('c', self.cc_cnt)
            inc = (self.cc_sem, None)
        elif dma:
            idx = self.dma_cnt[eng]
            self.dma_cnt[eng] += 1
            tok = ('d', eng, idx)
            inc = (self.dma_sem[eng][idx % self.NDMA], 16)
        else:
            self.prog_cnt[eng] += 1
            tok = ('p', eng, self.prog_cnt[eng])
            inc = (self.prog_sem[eng], 1)
        self.q[eng].append((waits, fn, inc))
        for r in reads:
            self.res_r.setdefault(r, []).append(tok)
        if acc in (None, 'stop'):
            for w in writes:
                self.res_w[w] = tok
                self.res_r[w] = []
        return tok

    def frontier(self):
        toks = [('p', e, self.prog_cnt[e]) for e in self.COMPUTE
                if self.prog_cnt[e] > 0]
        for q in self.QUEUES:
            n = self.dma_cnt[q]
            for slot in range(min(n, self.NDMA)):
                nslot = (n - 1 - slot) // self.NDMA + 1
                toks.append(('d', q, slot + (nslot - 1) * self.NDMA))
        if self.cc_cnt > 0:
            toks.append(('c', self.cc_cnt))
        return toks

    def barrier(self):
        toks = self.frontier()
        for e in self.q:
            waits = []
            for t in toks:
                if t[0] == 'p' and t[1] == e:
                    continue
                wv = self._need(e, t)
                if wv:
                    waits.append(wv)
            if waits:
                self.q[e].append((waits, None, None))

    def finalize(self):
        waits = []
        for t in self.frontier():
            wv = self._need('sync', t)
            if wv:
                waits.append(wv)
        for qn in self.QUEUES:
            n = self.dma_cnt[qn]
            for slot in range(min(n, self.NDMA)):
                nslot = (n - 1 - slot) // self.NDMA + 1
                wv = self._need('sync', ('d', qn, slot + (nslot - 1) * self.NDMA))
                if wv:
                    waits.append(wv)
        if waits:
            self.q['sync'].append((waits, None, None))

    def run(self, block):
        def runner(name):
            def go(eng):
                for waits, fn, inc in self.q[name]:
                    for sem, val in waits:
                        eng.wait_ge(sem, val)
                    if fn is not None:
                        ins = fn(eng)
                        if inc is not None:
                            if inc[1] is None:
                                ins.then_inc(inc[0])
                            else:
                                ins.then_inc(inc[0], inc[1])
            return go
        block.tensor(runner('tensor'))
        block.scalar(runner('scalar'))
        block.vector(runner('vector'))
        block.gpsimd(runner('gpsimd'))
        block.sync(runner('sync'))


def build_nc(dbg=DEBUG):
    nc = bass.Bass("TRN2", target_bir_lowering=False)
    s = Sched(nc)
    NGRP = dbg.get("n_groups", 8)

    d_x = nc.dram_tensor("x", [D, TOK], F32, kind="ExternalInput").ap()
    d_wqkv = nc.dram_tensor("wqkv", [28, 128, 16, 128], F32, kind="ExternalInput").ap()
    d_wk = nc.dram_tensor("wk", [NKV, 128, 4, 192], F32, kind="ExternalInput").ap()
    d_wv = nc.dram_tensor("wv", [NKV, 128, 4, 128], F32, kind="ExternalInput").ap()
    d_wo = nc.dram_tensor("wo", [16, 128, 16, 128], F32, kind="ExternalInput").ap()
    d_w1 = nc.dram_tensor("w1", [NE, FC, 128, DC, 128], F32, kind="ExternalInput").ap()
    d_w2 = nc.dram_tensor("w2", [NE, DC, 128, FC, 128], F32, kind="ExternalInput").ap()
    d_b1 = nc.dram_tensor("b1", [128, NE, FC], F32, kind="ExternalInput").ap()
    d_b2sh = nc.dram_tensor("b2sh", [NSH, D], F32, kind="ExternalInput").ap()
    d_b2r = nc.dram_tensor("b2r", [NR, D], F32, kind="ExternalInput").ap()
    d_cent = nc.dram_tensor("cent", [128, DC, NR], F32, kind="ExternalInput").ap()
    d_gbias = nc.dram_tensor("gbias", [1, NR], F32, kind="ExternalInput").ap()
    d_qcos = nc.dram_tensor("qcos", [D_ROPE, TOK], F32, kind="ExternalInput").ap()
    d_qsin = nc.dram_tensor("qsin", [D_ROPE, TOK], F32, kind="ExternalInput").ap()
    d_kcos = nc.dram_tensor("kcos", [D_ROPE, S], F32, kind="ExternalInput").ap()
    d_ksin = nc.dram_tensor("ksin", [D_ROPE, S], F32, kind="ExternalInput").ap()
    d_mask = nc.dram_tensor("mask", [128, 512], F32, kind="ExternalInput").ap()
    d_maskt = nc.dram_tensor("maskt", [4, 128, 128], F32, kind="ExternalInput").ap()
    d_ident = nc.dram_tensor("ident", [128, 128], F32, kind="ExternalInput").ap()
    d_out = nc.dram_tensor("out", [D, TOK], F32, kind="ExternalOutput").ap()

    d_qspill = nc.dram_tensor("qspill", [QH, TOK], F32, kind="Internal").ap()
    d_ospill = nc.dram_tensor("ospill", [D, TOK], BF16, kind="Internal").ap()
    d_kswap = nc.dram_tensor("kswap", [D_ROPE, S], BF16, kind="Internal").ap()
    d_gspill = nc.dram_tensor("gspill", [NR, TOK], BF16, kind="Internal").ap()
    d_cin = nc.dram_tensor("cin", [LORA, TOK], BF16, kind="Internal").ap()
    d_cout = nc.dram_tensor("cout", [4 * LORA, TOK], BF16, kind="Internal").ap()

    T = {}

    def sb(name, shape, dt):
        T[name] = nc.alloc_sbuf_tensor(name, shape, dt)

    for i in range(DC):
        sb(f"x{i}", [128, TOK], F32)
    sb("ident_bf", [128, 128], BF16)
    sb("ident_f", [128, 128], F32)
    sb("onesc_f", [128, 1], F32)
    sb("onesc_b", [128, 1], BF16)
    sb("ones1_f", [1, 128], F32)
    sb("ones1_b", [1, 128], BF16)
    sb("mcol", [128, 8], F32)
    sb("rscol", [128, 8], F32)
    sb("smallc", [128, 8], F32)
    sb("csum_b", [128, NR], F32)
    sb("biasr_b", [128, NR], F32)
    sb("csum_r", [1, NR], F32)
    sb("bm0", [128, 8], F32)
    sb("bm1", [128, 8], F32)
    sb("sx0", [128, 8], F32)
    sb("sx1", [128, 8], F32)
    sb("rcp0", [128, 8], F32)
    sb("rcp1", [128, 8], F32)
    sb("rci", [128, 4], F32)
    sb("nmx0", [128, 8], BF16)
    sb("nmx1", [128, 8], BF16)
    sb("gbias_r", [1, NR], F32)

    marker = nc.alloc_sbuf_tensor("ovl_marker", [128, 16], F32)
    ovl_base = nc.lookup_mloc("ovl_marker").addr + 64

    def ov(name, shape, dt, off):
        T[name] = nc.alloc_sbuf_tensor_at(name, shape, dt, offset=ovl_base + off)

    o = 0
    for i in range(DC):
        ov(f"xn{i}", [128, TOK], BF16, o + 2048 * i)
    o += 2048 * DC
    for b in range(2):
        for fb in range(FC):
            ov(f"h{b}_{fb}", [128, TOK], BF16, o + (b * FC + fb) * 2048)
    o += 2 * FC * 2048
    ov("xsq0", [128, TOK], BF16, o); o += 2048
    ov("xsq1", [128, TOK], BF16, o); o += 2048
    ov("gb0", [128, TOK], BF16, o); o += 2048
    ov("gb1", [128, TOK], BF16, o); o += 2048
    ov("m_b", [128, TOK], F32, o); o += 4096
    ov("rs_b", [128, TOK], F32, o); o += 4096
    ov("xtmp", [128, TOK], F32, o); o += 4096
    ov("m_row", [1, TOK], F32, o); o += 4096
    ov("rs_row", [1, TOK], F32, o); o += 4096
    NW1 = 3
    for b in range(NW1):
        ov(f"w1s{b}", [128, 2048], BF16, o + b * 4096)
    o += NW1 * 4096
    NW2 = 3
    for b in range(NW2):
        ov(f"w2s{b}", [128, FC * 128], BF16, o + b * 2816)
    o += NW2 * 2816
    ov("b2sh", [NSH, D], BF16, o); o += 4096
    ov("b2r", [NR, D], BF16, o); o += 4096
    ov("ones2", [NSH, TOK], BF16, o); o += 2048
    ov("b1t", [128, NE * FC], F32, o); o += 800
    ov("cent", [128, DC * NR], F32, o); o += 1024
    ov("G_bf", [NR, TOK], BF16, o); o += 2048
    ov("g_row", [1, TOK], BF16, o); o += 2048
    ov("Dt", [128, NR], F32, o); o += 64
    ov("sc0", [128, NR], F32, o); o += 64
    ov("sc1", [128, NR], F32, o); o += 64
    ov("w0", [128, NR], F32, o); o += 64
    ov("w1t", [128, NR], F32, o); o += 64
    ov("gat", [128, NR], F32, o); o += 64
    ov("gat_b", [128, NR], BF16, o); o += 32
    ffn_end = o

    ov("S0", [128, S], F32, 0)
    ov("S1", [128, S], F32, 16384)
    NPT = 8
    for p_ in range(NPT):
        ov(f"pt{p_}", [128, 512], BF16, 32768 + p_ * 1024)
    ov("ks", [D_ROPE, S], BF16, 40960)
    ov("rcb", [128, 512], F32, 49152)
    ov("nmrow", [1, 512], BF16, 51200)
    ov("rcrow", [1, 512], F32, 52224)
    ov("srow", [1, 512], F32, 56320)
    ov("maskt", [128, 512], F32, 54272)
    for c in range(4):
        ov(f"ckvl{c}", [128, TOK], BF16, 65536 + c * 2048)
    ov("qst0", [128, TOK], F32, 73728)
    ov("qst1", [128, TOK], F32, 77824)
    ov("kn", [128, S], BF16, 65536)
    ov("krr", [D_ROPE + 1, S], BF16, 73728)
    ov("cs0", [128, 2048], BF16, 81920)
    ov("cs1", [128, 2048], BF16, 86016)
    ov("Vh", [128, S], BF16, 90112)
    a = 98304
    ov("qn0", [128, TOK], BF16, a); a += 2048
    ov("qn1", [128, TOK], BF16, a); a += 2048
    ov("qr0", [D_ROPE + 1, TOK], BF16, a); a += 2048
    ov("qr1", [D_ROPE + 1, TOK], BF16, a); a += 2048
    ov("qs", [D_ROPE, TOK], BF16, a); a += 2048
    ov("kcos", [D_ROPE, S], BF16, a); a += 8192
    ov("ksin", [D_ROPE, S], BF16, a); a += 8192
    ov("qcos", [D_ROPE, TOK], BF16, a); a += 2048
    ov("qsin", [D_ROPE, TOK], BF16, a); a += 2048
    ov("maskf", [128, 512], F32, a); a += 2048
    for b in range(2):
        ov(f"wks{b}", [128, 4 * 192], BF16, a + b * 1536)
    a += 2 * 1536
    for b in range(2):
        ov(f"wvs{b}", [128, 4 * 128], BF16, a + b * 1024)
    a += 2 * 1024
    for b in range(2):
        ov(f"ost{b}", [128, 512], BF16, a + b * 1024)
    a += 2048
    ov("gmx0", [128, 512], F32, a); a += 2048
    ov("gmx1", [128, 512], F32, a); a += 2048
    for kb in range(16):
        ov(f"Ores{kb}", [128, TOK], BF16, kb * 2048)
    for b in range(2):
        ov(f"wos{b}", [128, 2048], BF16, 32768 + b * 4096)
    attn_end = a

    ovl_size = max(ffn_end, attn_end)
    assert ovl_base + ovl_size <= 224 * 1024 - 256, \
        f"SBUF overflow: base={ovl_base} size={ovl_size}"

    P = {}
    P['hp0'] = nc.alloc_psum_tensor("hp0", [128, 1024], F32)
    P['hp1'] = nc.alloc_psum_tensor("hp1", [128, 1024], F32)
    P['yp0'] = nc.alloc_psum_tensor("yp0", [128, 512], F32)
    P['yp1'] = nc.alloc_psum_tensor("yp1", [128, 512], F32)
    P['gp'] = nc.alloc_psum_tensor("gp", [128, 1024], F32)
    gp_bf = P['gp'][:].bitcast(BF16)

    def hpk(b, half):
        return f"PB{b * 2 + half}"
    YP0, YP1, GPA, GPB = "PB4", "PB5", "PB6", "PB7"

    def dma(qname, fn, reads=(), writes=()):
        return s.op(qname, fn, reads=reads, writes=writes, dma=True)

    def op(eng, fn, reads=(), writes=(), acc=None):
        return s.op(eng, fn, reads=reads, writes=writes, acc=acc)

    def acctag(i, n):
        return 'start' if i == 0 else ('stop' if i == n - 1 else 'mid')

    for i in range(DC):
        dma('sync', (lambda t_, a_: lambda e: e.dma_start(t_[:], a_))(
            T[f"x{i}"], d_x[128 * i:128 * (i + 1), :]), writes=[f"x{i}"])
    dma('gpsimd', lambda e: e.dma_start(T["ident_bf"][:], d_ident), writes=["ident_bf"])
    dma('sync', lambda e: e.dma_start(T["ident_f"][:], d_ident), writes=["ident_f"])
    op('vector', lambda e: e.memset(T["onesc_f"][:], 1.0), writes=["onesc_f"])
    op('vector', lambda e: e.memset(T["onesc_b"][:], 1.0), writes=["onesc_b"])
    op('vector', lambda e: e.memset(T["ones1_f"][:], 1.0), writes=["ones1_f"])
    op('vector', lambda e: e.memset(T["ones1_b"][:], 1.0), writes=["ones1_b"])

    def ffn_const_dmas():
        dma('sync', lambda e: e.dma_start(T["cent"][:],
                                          d_cent.rearrange("p c e -> p (c e)")),
            writes=["cent"])
        dma('sync', lambda e: e.dma_start(T["b1t"][:, 0:NE * FC],
                                          d_b1.rearrange("p e c -> p (e c)")),
            writes=["b1t"])
        dma('gpsimd', lambda e: e.dma_start(T["b2sh"][:], d_b2sh), writes=["b2sh"])
        dma('gpsimd', lambda e: e.dma_start(T["b2r"][:], d_b2r), writes=["b2r"])
        dma('sync', lambda e: e.dma_start(T["gbias_r"][:], d_gbias), writes=["gbias_r"])

    def ffn_const_compute():
        op('vector', lambda e: e.memset(T["ones2"][:], 1.0), writes=["ones2"])
        for kc in range(DC):
            op('tensor', (lambda kc_: lambda e: e.matmul(
                P['hp0'][0:1, 0:NR], T["onesc_f"][:],
                T["cent"][:, kc_ * NR:(kc_ + 1) * NR],
                start=(kc_ == 0), stop=(kc_ == DC - 1)))(kc),
                reads=["cent", "onesc_f"], writes=[hpk(0, 0)], acc=acctag(kc, DC))
        op('vector', lambda e: e.tensor_copy(T["csum_r"][:], P['hp0'][0:1, 0:NR]),
           reads=[hpk(0, 0)], writes=["csum_r"])
        op('tensor', lambda e: e.matmul(P['yp0'][:, 0:NR], T["ones1_f"][:],
                                        T["csum_r"][:], start=True, stop=True),
           reads=["ones1_f", "csum_r"], writes=[YP0])
        op('vector', lambda e: e.tensor_copy(T["csum_b"][:], P['yp0'][:, 0:NR]),
           reads=[YP0], writes=["csum_b"])
        op('tensor', lambda e: e.matmul(P['yp1'][:, 0:NR], T["ones1_f"][:],
                                        T["gbias_r"][:], start=True, stop=True),
           reads=["ones1_f", "gbias_r"], writes=[YP1])
        op('vector', lambda e: e.tensor_copy(T["biasr_b"][:], P['yp1'][:, 0:NR]),
           reads=[YP1], writes=["biasr_b"])

    def emit_ln(with_flat):
        for i in range(DC):
            xs = f"xsq{i % 2}"
            op('scalar', (lambda i_, xs_: lambda e: e.activation(
                T[xs_][:], T[f"x{i_}"][:], AF.Square))(i, xs),
                reads=[f"x{i}"], writes=[xs])
            for half in range(2):
                sl = slice(512 * half, 512 * (half + 1))
                op('tensor', (lambda i_, sl_: lambda e: e.matmul(
                    P['hp0'][0:1, sl_], T["onesc_f"][:], T[f"x{i_}"][:, sl_],
                    start=(i_ == 0), stop=(i_ == DC - 1)))(i, sl),
                    reads=[f"x{i}", "onesc_f"], writes=[hpk(0, half)],
                    acc=acctag(i, DC))
                op('tensor', (lambda sl_, xs_, i_: lambda e: e.matmul(
                    P['hp1'][0:1, sl_], T["onesc_b"][:], T[xs_][:, sl_],
                    start=(i_ == 0), stop=(i_ == DC - 1)))(sl, xs, i),
                    reads=[xs, "onesc_b"], writes=[hpk(1, half)],
                    acc=acctag(i, DC))
        op('scalar', lambda e: e.activation(T["m_row"][:], P['hp0'][0:1, :],
                                            AF.Copy, scale=1.0 / D),
           reads=[hpk(0, 0), hpk(0, 1)], writes=["m_row"])
        op('scalar', lambda e: e.activation(T["rs_row"][:], P['hp1'][0:1, :],
                                            AF.Copy, scale=1.0 / D),
           reads=[hpk(1, 0), hpk(1, 1)], writes=["rs_row"])
        op('vector', lambda e: e.tensor_mul(T["xtmp"][0:1, :], T["m_row"][:],
                                            T["m_row"][:]),
           reads=["m_row"], writes=["xtmp"])
        op('vector', lambda e: e.tensor_sub(T["rs_row"][:], T["rs_row"][:],
                                            T["xtmp"][0:1, :]),
           reads=["rs_row", "xtmp"], writes=["rs_row"])
        op('vector', lambda e: e.tensor_scalar(T["rs_row"][:], T["rs_row"][:],
                                               EPS, None, ALU.add),
           reads=["rs_row"], writes=["rs_row"])
        op('scalar', lambda e: e.activation(T["rs_row"][:], T["rs_row"][:], AF.Ln),
           reads=["rs_row"], writes=["rs_row"])
        op('scalar', lambda e: e.activation(T["rs_row"][:], T["rs_row"][:],
                                            AF.Exp, scale=-0.5),
           reads=["rs_row"], writes=["rs_row"])
        for half in range(2):
            sl = slice(512 * half, 512 * (half + 1))
            yp = P['yp0'] if half == 0 else P['yp1']
            ypk = YP0 if half == 0 else YP1
            gpk = GPA if half == 0 else GPB
            op('tensor', (lambda sl_, yp_: lambda e: e.matmul(
                yp_[:], T["ones1_f"][:], T["m_row"][0:1, sl_],
                start=True, stop=True))(sl, yp),
                reads=["ones1_f", "m_row"], writes=[ypk])
            op('vector', (lambda sl_, yp_: lambda e: e.tensor_copy(
                T["m_b"][:, sl_], yp_[:]))(sl, yp),
                reads=[ypk], writes=["m_b"])
            op('tensor', (lambda sl_: lambda e: e.matmul(
                P['gp'][:, sl_], T["ones1_f"][:], T["rs_row"][0:1, sl_],
                start=True, stop=True))(sl),
                reads=["ones1_f", "rs_row"], writes=[gpk])
            op('vector', (lambda sl_: lambda e: e.tensor_copy(
                T["rs_b"][:, sl_], P['gp'][:, sl_]))(sl),
                reads=[gpk], writes=["rs_b"])
        for i in range(DC):
            op('vector', (lambda i_: lambda e: e.tensor_sub(
                T["xtmp"][:], T[f"x{i_}"][:], T["m_b"][:]))(i),
                reads=[f"x{i}", "m_b"], writes=["xtmp"])
            op('vector', (lambda i_: lambda e: e.tensor_mul(
                T[f"xn{i_}"][:], T["xtmp"][:], T["rs_b"][:]))(i),
                reads=["xtmp", "rs_b"], writes=[f"xn{i}"])
        del with_flat

    def emit_flat():
        for i in range(DC):
            op('vector', (lambda i_: lambda e: e.scalar_tensor_tensor(
                T[f"x{i_}"][:], T[f"xn{i_}"][:], 0.5, T[f"x{i_}"][:],
                ALU.mult, ALU.add))(i),
                reads=[f"xn{i}", f"x{i}"], writes=[f"x{i}"])

    def emit_statcols():
        for c in range(8):
            op('tensor', (lambda c_: lambda e: e.transpose(
                P['yp0'][:, 0:1], T["m_row"][0:1, 128 * c_:128 * (c_ + 1)],
                T["ident_f"][0:1, 0:1]))(c),
                reads=["m_row", "ident_f"], writes=[YP0])
            op('vector', (lambda c_: lambda e: e.tensor_copy(
                T["mcol"][:, c_:c_ + 1], P['yp0'][:, 0:1]))(c),
                reads=[YP0], writes=["mcol"])
            op('tensor', (lambda c_: lambda e: e.transpose(
                P['yp1'][:, 0:1], T["rs_row"][0:1, 128 * c_:128 * (c_ + 1)],
                T["ident_f"][0:1, 0:1]))(c),
                reads=["rs_row", "ident_f"], writes=[YP1])
            op('vector', (lambda c_: lambda e: e.tensor_copy(
                T["rscol"][:, c_:c_ + 1], P['yp1'][:, 0:1]))(c),
                reads=[YP1], writes=["rscol"])

    def emit_gating():
        emit_statcols()
        for tt in range(8):
            for kc in range(DC):
                op('tensor', (lambda tt_, kc_: lambda e: e.matmul(
                    P['yp0'][:, 0:NR], T[f"x{kc_}"][:, 128 * tt_:128 * (tt_ + 1)],
                    T["cent"][:, kc_ * NR:(kc_ + 1) * NR],
                    start=(kc_ == 0), stop=(kc_ == DC - 1)))(tt, kc),
                    reads=[f"x{kc}", "cent"], writes=[YP0], acc=acctag(kc, DC))
            op('vector', lambda e: e.tensor_copy(T["Dt"][:], P['yp0'][:, 0:NR]),
               reads=[YP0], writes=["Dt"])
            op('vector', (lambda tt_: lambda e: e.tensor_scalar(
                T["sc0"][:], T["csum_b"][:], T["mcol"][:, tt_:tt_ + 1], None,
                ALU.mult))(tt),
                reads=["csum_b", "mcol"], writes=["sc0"])
            op('vector', lambda e: e.tensor_sub(T["sc1"][:], T["Dt"][:], T["sc0"][:]),
               reads=["Dt", "sc0"], writes=["sc1"])
            op('vector', (lambda tt_: lambda e: e.tensor_scalar(
                T["sc1"][:], T["sc1"][:], T["rscol"][:, tt_:tt_ + 1], None,
                ALU.mult))(tt),
                reads=["sc1", "rscol"], writes=["sc1"])
            op('scalar', lambda e: e.activation(T["sc0"][:], T["sc1"][:], AF.Exp,
                                                scale=-1.0),
               reads=["sc1"], writes=["sc0"])
            op('vector', lambda e: e.tensor_scalar(T["sc0"][:], T["sc0"][:], 1.0,
                                                   None, ALU.add),
               reads=["sc0"], writes=["sc0"])
            op('vector', lambda e: e.reciprocal(T["sc0"][:], T["sc0"][:]),
               reads=["sc0"], writes=["sc0"])
            op('vector', lambda e: e.tensor_add(T["sc1"][:], T["sc0"][:],
                                                T["biasr_b"][:]),
               reads=["sc0", "biasr_b"], writes=["sc1"])
            op('vector', lambda e: e.tensor_copy(T["w0"][:], T["sc1"][:]),
               reads=["sc1"], writes=["w0"])
            for it in range(KR):
                op('vector', lambda e: e.reduce_max(T["smallc"][:, 0:1], T["w0"][:],
                                                    axis=AX.X),
                   reads=["w0"], writes=["smallc"])
                if it < KR - 1:
                    op('vector', lambda e: e.tensor_scalar(
                        T["w1t"][:], T["w0"][:], T["smallc"][:, 0:1], None,
                        ALU.is_ge),
                        reads=["w0", "smallc"], writes=["w1t"])
                    op('vector', lambda e: e.scalar_tensor_tensor(
                        T["w0"][:], T["w1t"][:], NEG, T["w0"][:], ALU.mult,
                        ALU.add),
                        reads=["w1t", "w0"], writes=["w0"])
            op('vector', lambda e: e.tensor_scalar(
                T["w1t"][:], T["sc1"][:], T["smallc"][:, 0:1], None, ALU.is_ge),
                reads=["sc1", "smallc"], writes=["w1t"])
            op('vector', lambda e: e.tensor_mul(T["gat"][:], T["sc1"][:],
                                                T["w1t"][:]),
               reads=["sc1", "w1t"], writes=["gat"])
            op('vector', lambda e: e.reduce_sum(T["smallc"][:, 1:2], T["gat"][:],
                                                axis=AX.X),
               reads=["gat"], writes=["smallc"])
            op('vector', lambda e: e.tensor_scalar(
                T["smallc"][:, 1:2], T["smallc"][:, 1:2], 1e-8, None, ALU.max),
                reads=["smallc"], writes=["smallc"])
            op('vector', lambda e: e.reciprocal(T["smallc"][:, 2:3],
                                                T["smallc"][:, 1:2]),
               reads=["smallc"], writes=["smallc"])
            op('vector', lambda e: e.tensor_scalar(
                T["gat"][:], T["gat"][:], T["smallc"][:, 2:3], 0.5, ALU.mult,
                ALU.mult),
                reads=["gat", "smallc"], writes=["gat"])
            op('vector', lambda e: e.tensor_copy(T["gat_b"][:], T["gat"][:]),
               reads=["gat"], writes=["gat_b"])
            op('tensor', (lambda tt_: lambda e: e.transpose(
                gp_bf[0:NR, 128 * tt_:128 * (tt_ + 1)], T["gat_b"][:],
                T["ident_bf"][:]))(tt),
                reads=["gat_b", "ident_bf"], writes=[GPA])
        op('vector', lambda e: e.tensor_copy(T["G_bf"][:], gp_bf[0:NR, 0:TOK]),
           reads=[GPA], writes=["G_bf"])
        dma('sync', lambda e: e.dma_start(d_gspill, T["G_bf"][:]),
            reads=["G_bf"], writes=["gspill"])

    def emit_experts(mid_hook=None):
        groups = [(0, 1, True)] + [(2 + 2 * g, 3 + 2 * g, False)
                                   for g in range(NGRP)]
        w1slot = [0]
        w2slot = [0]

        def load_w1(e_idx, fb):
            b = w1slot[0]
            w1slot[0] = (b + 1) % NW1
            dma('gpsimd', (lambda e_, fb_, b_: lambda e: e.dma_start(
                T[f"w1s{b_}"][:].rearrange("p (a c) -> p a c", c=128),
                d_w1[e_, fb_]))(e_idx, fb, b),
                writes=[f"w1s{b}"])
            return b

        def load_w2(e_idx, dm):
            b = w2slot[0]
            w2slot[0] = (b + 1) % NW2
            dma('gpsimd', (lambda e_, dm_, b_: lambda e: e.dma_start(
                T[f"w2s{b_}"][:].rearrange("p (a c) -> p a c", c=128),
                d_w2[e_, dm_]))(e_idx, dm, b),
                writes=[f"w2s{b}"])
            return b

        def prep_gb(e_idx, hb):
            r = e_idx - NSH
            dma('sync', (lambda r_: lambda e: e.dma_start(
                T["g_row"][:], d_gspill[r_:r_ + 1, :]))(r),
                reads=["gspill"], writes=["g_row"])
            for half in range(2):
                sl = slice(512 * half, 512 * (half + 1))
                op('tensor', (lambda sl_: lambda e: e.matmul(
                    P['gp'][:, sl_], T["ones1_b"][:], T["g_row"][0:1, sl_],
                    start=True, stop=True))(sl),
                    reads=["ones1_b", "g_row"],
                    writes=[GPA if half == 0 else GPB])
            op('vector', (lambda hb_: lambda e: e.tensor_copy(
                T[f"gb{hb_}"][:], P['gp'][:]))(hb),
                reads=[GPA, GPB], writes=[f"gb{hb}"])

        def emit_h(e_idx, shared):
            hb = e_idx % 2
            if not shared:
                prep_gb(e_idx, hb)
            for fb in range(FC):
                wb = load_w1(e_idx, fb)
                pb = fb % 2
                hp = P[f'hp{pb}']
                for kc in range(DC):
                    for half in range(2):
                        sl = slice(512 * half, 512 * (half + 1))
                        op('tensor', (lambda wb_, kc_, sl_, hp_, i_: lambda e: e.matmul(
                            hp_[:, sl_],
                            T[f"w1s{wb_}"][:, 128 * kc_:128 * (kc_ + 1)],
                            T[f"xn{kc_}"][:, sl_],
                            start=(i_ == 0), stop=(i_ == DC - 1)))(wb, kc, sl, hp, kc),
                            reads=[f"w1s{wb}", f"xn{kc}"], writes=[hpk(pb, half)],
                            acc=acctag(kc, DC))
                for half in range(2):
                    sl = slice(512 * half, 512 * (half + 1))
                    op('scalar', (lambda e_, fb_, sl_, hb_, hp_: lambda e: e.activation(
                        T[f"h{hb_}_{fb_}"][:, sl_], hp_[:, sl_], AF.Relu,
                        bias=T["b1t"][:, e_ * FC + fb_:e_ * FC + fb_ + 1]))(
                            e_idx, fb, sl, hb, P[f'hp{pb}']),
                        reads=[hpk(pb, half), "b1t"], writes=[f"h{hb}_{fb}"])
                if not shared:
                    op('vector', (lambda fb_, hb_: lambda e: e.tensor_mul(
                        T[f"h{hb_}_{fb_}"][:], T[f"h{hb_}_{fb_}"][:],
                        T[f"gb{hb_}"][:]))(fb, hb),
                        reads=[f"h{hb}_{fb}", f"gb{hb}"], writes=[f"h{hb}_{fb}"])

        def emit_y_group(e0, e1, shared, first_routed):
            for dm in range(DC):
                wb0 = load_w2(e0, dm)
                wb1 = load_w2(e1, dm)
                yp = P['yp0'] if dm % 2 == 0 else P['yp1']
                ypk = YP0 if dm % 2 == 0 else YP1
                for half in range(2):
                    sl = slice(512 * half, 512 * (half + 1))
                    mms = [(e0, wb0, fb) for fb in range(FC)] + \
                          [(e1, wb1, fb) for fb in range(FC)]
                    extra = 1 if (shared or first_routed) else 0
                    total = len(mms) + extra
                    for idx, (ee, wb, fb) in enumerate(mms):
                        op('tensor', (lambda wb_, fb_, ee_, sl_, yp_, i_, n_:
                                      lambda e: e.matmul(
                            yp_[:], T[f"w2s{wb_}"][:, 128 * fb_:128 * (fb_ + 1)],
                            T[f"h{ee_ % 2}_{fb_}"][:, sl_],
                            start=(i_ == 0), stop=(i_ == n_ - 1)))(
                                wb, fb, ee, sl, yp, idx, total),
                            reads=[f"w2s{wb}", f"h{ee % 2}_{fb}"], writes=[ypk],
                            acc=acctag(idx, total))
                    if shared:
                        op('tensor', (lambda dm_, sl_, yp_: lambda e: e.matmul(
                            yp_[:], T["b2sh"][:, 128 * dm_:128 * (dm_ + 1)],
                            T["ones2"][:, sl_], start=False, stop=True))(dm, sl, yp),
                            reads=["b2sh", "ones2"], writes=[ypk], acc='stop')
                    elif first_routed:
                        op('tensor', (lambda dm_, sl_, yp_: lambda e: e.matmul(
                            yp_[:], T["b2r"][:, 128 * dm_:128 * (dm_ + 1)],
                            T["G_bf"][:, sl_], start=False, stop=True))(dm, sl, yp),
                            reads=["b2r", "G_bf"], writes=[ypk], acc='stop')
                    if shared:
                        op('vector', (lambda dm_, sl_, yp_: lambda e:
                                      e.scalar_tensor_tensor(
                            T[f"x{dm_}"][:, sl_], yp_[:], 0.5,
                            T[f"x{dm_}"][:, sl_], ALU.mult, ALU.add))(dm, sl, yp),
                            reads=[ypk, f"x{dm}"], writes=[f"x{dm}"])
                    else:
                        op('vector', (lambda dm_, sl_, yp_: lambda e: e.tensor_add(
                            T[f"x{dm_}"][:, sl_], T[f"x{dm_}"][:, sl_], yp_[:]))(
                                dm, sl, yp),
                            reads=[ypk, f"x{dm}"], writes=[f"x{dm}"])
        emit_h(0, True)
        emit_h(1, True)
        if mid_hook is not None:
            mid_hook()
        emit_y_group(0, 1, True, False)
        for gi, (e0, e1, shared) in enumerate(groups[1:]):
            emit_h(e0, shared)
            emit_h(e1, shared)
            emit_y_group(e0, e1, False, gi == 0)

    def emit_ffn(skip_const_dmas=False):
        if not skip_const_dmas:
            ffn_const_dmas()
        ffn_const_compute()
        emit_ln(with_flat=True)
        emit_experts(mid_hook=lambda: (emit_gating(), emit_flat()))

    def emit_qkv(order):
        wslot = [0]
        for nb in order:
            b = wslot[0]
            wslot[0] = 1 - b
            dma('gpsimd', (lambda nb_, b_: lambda e: e.dma_start(
                T[f"w1s{b_}"][:].rearrange("p (a c) -> p a c", c=128),
                d_wqkv[nb_]))(nb, b),
                writes=[f"w1s{b}"])
            pb = nb % 2
            hp = P[f'hp{pb}']
            for kc in range(DC):
                for half in range(2):
                    sl = slice(512 * half, 512 * (half + 1))
                    op('tensor', (lambda b_, kc_, sl_, hp_: lambda e: e.matmul(
                        hp_[:, sl_], T[f"w1s{b_}"][:, 128 * kc_:128 * (kc_ + 1)],
                        T[f"xn{kc_}"][:, sl_],
                        start=(kc_ == 0), stop=(kc_ == DC - 1)))(b, kc, sl, hp),
                        reads=[f"w1s{b}", f"xn{kc}"], writes=[hpk(pb, half)],
                        acc=acctag(kc, DC))
            if nb < 24:
                st = f"qst{nb % 2}"
                op('scalar', (lambda st_, hp_: lambda e: e.activation(
                    T[st_][:], hp_[:], AF.Copy, scale=SCALE))(st, P[f'hp{pb}']),
                    reads=[hpk(pb, 0), hpk(pb, 1)], writes=[st])
                dma('sync', (lambda nb_, st_: lambda e: e.dma_start(
                    d_qspill[128 * nb_:128 * (nb_ + 1), :], T[st_][:]))(nb, st),
                    reads=[st], writes=[f"qsp{nb}"])
            else:
                c = nb - 24
                op('scalar', (lambda c_, hp_: lambda e: e.activation(
                    T[f"ckvl{c_}"][:], hp_[:], AF.Copy))(c, P[f'hp{pb}']),
                    reads=[hpk(pb, 0), hpk(pb, 1)], writes=[f"ckvl{c}"])
                dma('sync', (lambda c_: lambda e: e.dma_start(
                    d_cin[128 * c_:128 * (c_ + 1), :], T[f"ckvl{c_}"][:]))(c),
                    reads=[f"ckvl{c}"], writes=["cin"])

    def emit_collective():
        if dbg.get("mock_collective"):
            for r in range(4):
                dma('gpsimd', (lambda r_: lambda e: e.dma_start(
                    d_cout[LORA * r_:LORA * (r_ + 1), :], d_cin))(r),
                    reads=["cin"], writes=["cout"])
        else:
            s.op('gpsimd', lambda e: e.collective_compute(
                "AllGather", ALU.bypass,
                replica_groups=[[0, 1, 2, 3], [4, 5, 6, 7]],
                ins=[d_cin], outs=[d_cout]),
                reads=["cin"], writes=["cout"], coll=True)

    def emit_kv(kvh, wb):
        dma('gpsimd', (lambda kvh_, wb_: lambda e: e.dma_start(
            T[f"wks{wb_}"][:].rearrange("p (a c) -> p a c", c=192),
            d_wk[kvh_]))(kvh, wb),
            writes=[f"wks{wb}"])
        dma('gpsimd', (lambda kvh_, wb_: lambda e: e.dma_start(
            T[f"wvs{wb_}"][:].rearrange("p (a c) -> p a c", c=128),
            d_wv[kvh_]))(kvh, wb),
            writes=[f"wvs{wb}"])
        cout_r = d_cout.rearrange("(j c2 p) t -> p j c2 t", p=128, j=4)
        for nt in range(8):
            cb = nt % 2
            cs = f"cs{cb}"
            for kb in range(4):
                dst = T[cs][:, 512 * kb:512 * (kb + 1)].rearrange(
                    "p (j o) -> p j o", o=128)
                srcap = cout_r[:, :, kb, 128 * nt:128 * (nt + 1)]
                dma('sync', (lambda d2, s2: lambda e: e.dma_start(d2, s2))(dst, srcap),
                    reads=["cout"], writes=[cs])
            sl = slice(512 * nt, 512 * (nt + 1))
            yp = P['yp0'] if nt % 2 == 0 else P['yp1']
            ypk = YP0 if nt % 2 == 0 else YP1
            for kb in range(4):
                op('tensor', (lambda wb_, kb_, yp_, cs_: lambda e: e.matmul(
                    yp_[:], T[f"wks{wb_}"][:, 192 * kb_:192 * kb_ + 128],
                    T[cs_][:, 512 * kb_:512 * (kb_ + 1)],
                    start=(kb_ == 0), stop=(kb_ == 3)))(wb, kb, yp, cs),
                    reads=[f"wks{wb}", cs], writes=[ypk], acc=acctag(kb, 4))
            op('scalar', (lambda sl_, yp_: lambda e: e.activation(
                T["kn"][:, sl_], yp_[:], AF.Copy))(sl, yp),
                reads=[ypk], writes=["kn"])
            for kb in range(4):
                op('tensor', (lambda wb_, kb_, yp_, cs_: lambda e: e.matmul(
                    yp_[0:64, :],
                    T[f"wks{wb_}"][:, 192 * kb_ + 128:192 * (kb_ + 1)],
                    T[cs_][:, 512 * kb_:512 * (kb_ + 1)],
                    start=(kb_ == 0), stop=(kb_ == 3)))(wb, kb, yp, cs),
                    reads=[f"wks{wb}", cs], writes=[ypk], acc=acctag(kb, 4))
            op('scalar', (lambda sl_, yp_: lambda e: e.activation(
                T["krr"][0:64, sl_], yp_[0:64, :], AF.Copy))(sl, yp),
                reads=[ypk], writes=["krr"])
            for tc in range(4):
                gh = P['gp'][:, 0:128] if tc % 2 == 0 else P['gp'][:, 512:640]
                gk = GPA if tc % 2 == 0 else GPB
                for kb in range(4):
                    op('tensor', (lambda wb_, kb_, tc_, gh_, cs_: lambda e: e.matmul(
                        gh_, T[cs_][:, 512 * kb_ + 128 * tc_:512 * kb_ + 128 * (tc_ + 1)],
                        T[f"wvs{wb_}"][:, 128 * kb_:128 * (kb_ + 1)],
                        start=(kb_ == 0), stop=(kb_ == 3)))(wb, kb, tc, gh, cs),
                        reads=[f"wvs{wb}", cs], writes=[gk], acc=acctag(kb, 4))
                op('scalar', (lambda nt_, tc_, gh_: lambda e: e.activation(
                    T["Vh"][:, 128 * (4 * nt_ + tc_):128 * (4 * nt_ + tc_ + 1)],
                    gh_, AF.Copy))(nt, tc, gh),
                    reads=[gk], writes=["Vh"])
        dma('sync', lambda e: e.dma_start(d_kswap, T["krr"][0:64, :]),
            reads=["krr"], writes=["kswap"])
        dma('sync', lambda e: e.dma_start(T["ks"][0:32, :], d_kswap[32:64, :]),
            reads=["kswap"], writes=["ks"])
        dma('sync', lambda e: e.dma_start(T["ks"][32:64, :], d_kswap[0:32, :]),
            reads=["kswap"], writes=["ks"])
        op('vector', lambda e: e.tensor_mul(T["krr"][0:64, :], T["krr"][0:64, :],
                                            T["kcos"][:]),
           reads=["krr", "kcos"], writes=["krr"])
        op('vector', lambda e: e.tensor_mul(T["ks"][:], T["ks"][:], T["ksin"][:]),
           reads=["ks", "ksin"], writes=["ks"])
        op('vector', lambda e: e.tensor_add(T["krr"][0:64, :], T["krr"][0:64, :],
                                            T["ks"][:]),
           reads=["krr", "ks"], writes=["krr"])
        op('vector', lambda e: e.memset(T["krr"][64:65, :], 1.0), writes=["krr"])

    def emit_qload(h, slot):
        lo, hi = 192 * h, 192 * h + 192
        qsp_reads = [f"qsp{nb}" for nb in range(lo // 128, (hi - 1) // 128 + 1)]
        dma('gpsimd', (lambda h_, s_: lambda e: e.dma_start(
            T[f"qn{s_}"][:], d_qspill[192 * h_:192 * h_ + 128, :]))(h, slot),
            reads=qsp_reads, writes=[f"qn{slot}"])
        dma('gpsimd', (lambda h_, s_: lambda e: e.dma_start(
            T[f"qr{s_}"][0:64, :], d_qspill[192 * h_ + 128:192 * h_ + 192, :]))(h, slot),
            reads=qsp_reads, writes=[f"qr{slot}"])
        dma('gpsimd', (lambda h_: lambda e: e.dma_start(
            T["qs"][0:32, :], d_qspill[192 * h_ + 160:192 * h_ + 192, :]))(h),
            reads=qsp_reads, writes=["qs"])
        dma('gpsimd', (lambda h_: lambda e: e.dma_start(
            T["qs"][32:64, :], d_qspill[192 * h_ + 128:192 * h_ + 160, :]))(h),
            reads=qsp_reads, writes=["qs"])
        op('vector', (lambda s_: lambda e: e.tensor_mul(
            T[f"qr{s_}"][0:64, :], T[f"qr{s_}"][0:64, :], T["qcos"][:]))(slot),
            reads=[f"qr{slot}", "qcos"], writes=[f"qr{slot}"])
        op('vector', lambda e: e.tensor_mul(T["qs"][:], T["qs"][:], T["qsin"][:]),
           reads=["qs", "qsin"], writes=["qs"])
        op('vector', (lambda s_: lambda e: e.tensor_add(
            T[f"qr{s_}"][0:64, :], T[f"qr{s_}"][0:64, :], T["qs"][:]))(slot),
            reads=[f"qr{slot}", "qs"], writes=[f"qr{slot}"])

    def p1_gen(h, slot):
        nmx = f"nmx{slot}"
        rcp = f"rcp{slot}"
        """Pass 1 of one head: scores S [q x k], per-row max (bf16-rounded),
        sum of exp (accum_out). Yields once per sub so the driver can
        interleave pass-2 tiles of the previous head."""
        qn, qr = f"qn{slot}", f"qr{slot}"
        sp_banks = [(P['hp0'], slice(0, 512), hpk(0, 0)),
                    (P['hp0'], slice(512, 1024), hpk(0, 1)),
                    (P['hp1'], slice(0, 512), hpk(1, 0)),
                    (P['hp1'], slice(512, 1024), hpk(1, 1))]
        for k in range(8):
            Lk = 512 * (k + 1)
            Sb = f"S{k % 2}"
            sx = f"sx{k % 2}"
            bm = f"bm{k % 2}"
            qsl = slice(128 * k, 128 * (k + 1))
            for blk in range(k + 1):
                hp_, psl, key = sp_banks[blk % 4]
                ksl = slice(512 * blk, 512 * (blk + 1))
                op('tensor', (lambda hp2, psl2, qsl2, ksl2: lambda e: e.matmul(
                    hp2[:, psl2], T[qn][:, qsl2], T["kn"][:, ksl2],
                    start=True, stop=False))(hp_, psl, qsl, ksl),
                    reads=[qn, "kn"], writes=[key], acc='start')
                op('tensor', (lambda hp2, psl2, qsl2, ksl2: lambda e: e.matmul(
                    hp2[:, psl2], T[qr][0:64, qsl2], T["krr"][0:64, ksl2],
                    start=False, stop=True))(hp_, psl, qsl, ksl),
                    reads=[qr, "krr"], writes=[key], acc='stop')
                if blk < k:
                    op('scalar', (lambda hp2, psl2, ksl2, Sb_: lambda e:
                                  e.activation(T[Sb_][:, ksl2], hp2[:, psl2],
                                               AF.Copy))(hp_, psl, ksl, Sb),
                        reads=[key], writes=[Sb])
                else:
                    op('vector', (lambda hp2, psl2, ksl2, Sb_: lambda e: e.tensor_add(
                        T[Sb_][:, ksl2], hp2[:, psl2], T["maskf"][:]))(
                            hp_, psl, ksl, Sb),
                        reads=[key, "maskf"], writes=[Sb])
                op('vector', (lambda b_, ksl2, Sb_, bm_: lambda e: e.reduce_max(
                    T[bm_][:, b_:b_ + 1], T[Sb_][:, ksl2], axis=AX.X))(
                        blk, ksl, Sb, bm),
                    reads=[Sb], writes=[bm])
            op('vector', (lambda k_, bm_, sx_: lambda e: e.reduce_max(
                T[sx_][:, 3:4], T[bm_][:, 0:k_ + 1], axis=AX.X, negate=True))(
                    k, bm, sx),
                reads=[bm], writes=[sx])
            op('vector', (lambda sx_, k_, nm_: lambda e: e.tensor_copy(
                T[nm_][:, k_:k_ + 1], T[sx_][:, 3:4]))(sx, k, nmx),
                reads=[sx], writes=[nmx])
            yield

    def p2_gen(h, slot):
        nmx = f"nmx{slot}"
        rcp = f"rcp{slot}"
        """Pass 2 of one head: S^T tiles (with -max as an augmented contraction
        row), Exp on evacuation into P^T, AV accumulation, 1/sum on the output.
        Yields between tiles."""
        qn, qr = f"qn{slot}", f"qr{slot}"
        hp_bf0 = P['hp0'][:].bitcast(BF16)
        for g in range(2):
            Lg = 512 * (4 * g + 4)
            nkc = Lg // 128
            rowyp = P['yp1'] if g % 2 == 0 else P['yp0']
            rowk = YP1 if g % 2 == 0 else YP0
            rowbf = rowyp[:].bitcast(BF16)
            for i in range(4):
                k = 4 * g + i
                op('tensor', (lambda k_, i_, rb_, nm_: lambda e: e.transpose(
                    rb_[0:1, 128 * i_:128 * (i_ + 1)],
                    T[nm_][:, k_:k_ + 1], T["ident_bf"][:]))(k, i, rowbf, nmx),
                    reads=[nmx, "ident_bf"], writes=[rowk])
            op('vector', (lambda rb_: lambda e: e.tensor_copy(
                T["nmrow"][:], rb_[0:1, 0:512]))(rowbf),
                reads=[rowk], writes=["nmrow"])
            dma('sync', (lambda g_, qr_: lambda e: e.dma_start(
                T[qr_][64:65, 512 * g_:512 * (g_ + 1)], T["nmrow"][:]))(g, qr),
                reads=["nmrow"], writes=[qr])
            yield
            yp = P['yp0'] if g % 2 == 0 else P['yp1']
            ypk = YP0 if g % 2 == 0 else YP1
            LA = 5
            pp_rot = [(P['gp'][:, 0:512], GPA), (P['gp'][:, 512:1024], GPB)]
            for it in range(nkc + LA):
                if it < nkc:
                    kc = it
                    slot2 = kc % NPT
                    pp, ppk = pp_rot[kc % 2]
                    kdiag = kc // 4
                    cst = 0 if kdiag < 4 * g else 128 * (kdiag - 4 * g)
                    stair = kdiag >= 4 * g
                    csl = slice(cst, 512)
                    qsl2 = slice(512 * g + cst, 512 * (g + 1))
                    kcs = slice(128 * kc, 128 * (kc + 1))
                    op('tensor', (lambda pp_, csl_, qsl_, kcs_: lambda e: e.matmul(
                        pp_[:, csl_], T["kn"][:, kcs_], T[qn][:, qsl_],
                        start=True, stop=False))(pp, csl, qsl2, kcs),
                        reads=["kn", qn], writes=[ppk], acc='start')
                    op('tensor', (lambda pp_, csl_, qsl_, kcs_: lambda e: e.matmul(
                        pp_[:, csl_], T["krr"][:, kcs_], T[qr][:, qsl_],
                        start=False, stop=True))(pp, csl, qsl2, kcs),
                        reads=["krr", qr], writes=[ppk], acc='stop')
                    if stair:
                        r = kc % 4
                        op('vector', (lambda pp_, cst_, r_: lambda e: e.tensor_add(
                            pp_[:, cst_:cst_ + 128], pp_[:, cst_:cst_ + 128],
                            T["maskt"][:, 128 * r_:128 * (r_ + 1)]))(pp, cst, r),
                            reads=[ppk, "maskt"], writes=[ppk])
                    op('scalar', (lambda pp_, csl_, sl_: lambda e: e.activation(
                        T[f"pt{sl_}"][:, csl_], pp_[:, csl_], AF.Exp))(
                            pp, csl, slot2),
                        reads=[ppk], writes=[f"pt{slot2}"])

                if it >= LA:
                    kc = it - LA
                    slot2 = kc % NPT
                    kdiag2 = kc // 4
                    cst2 = 0 if kdiag2 < 4 * g else 128 * (kdiag2 - 4 * g)
                    csl2 = slice(cst2, 512)
                    op('tensor', (lambda kc_, sl_, yp_, n_, c2: lambda e: e.matmul(
                        yp_[:, c2], T["Vh"][:, 128 * kc_:128 * (kc_ + 1)],
                        T[f"pt{sl_}"][:, c2],
                        start=(kc_ == 0), stop=(kc_ == n_ - 1)))(
                            kc, slot2, yp, nkc, csl2),
                        reads=["Vh", f"pt{slot2}"], writes=[ypk],
                        acc=acctag(kc, nkc))
                    op('tensor', (lambda ry_, c2, sl_, kc_, n_: lambda e: e.matmul(
                        ry_[0:1, c2], T["onesc_b"][:], T[f"pt{sl_}"][:, c2],
                        start=(kc_ == 0), stop=(kc_ == n_ - 1)))(
                            rowyp, csl2, slot2, kc, nkc),
                        reads=["onesc_b", f"pt{slot2}"], writes=[rowk],
                        acc=acctag(kc, nkc))
                yield
            op('vector', (lambda ry_: lambda e: e.tensor_copy(
                T["srow"][:], ry_[0:1, 0:512]))(rowyp),
                reads=[rowk], writes=["srow"])
            op('vector', lambda e: e.reciprocal(T["rcrow"][:], T["srow"][:]),
               reads=["srow"], writes=["rcrow"])
            op('tensor', (lambda ry_: lambda e: e.matmul(
                ry_[:], T["ones1_f"][:], T["rcrow"][:],
                start=True, stop=True))(rowyp),
                reads=["ones1_f", "rcrow"], writes=[rowk])
            op('vector', (lambda ry_: lambda e: e.tensor_copy(
                T["rcb"][:], ry_[:]))(rowyp),
                reads=[rowk], writes=["rcb"])
            ob = g % 2
            op('vector', (lambda ob_, yp_: lambda e: e.tensor_tensor(
                T[f"ost{ob_}"][:], yp_[:], T["rcb"][:], ALU.mult))(ob, yp),
                reads=[ypk, "rcb"], writes=[f"ost{ob}"])
            dma('sync', (lambda h_, g_, ob_: lambda e: e.dma_start(
                d_ospill[128 * h_:128 * (h_ + 1), 512 * g_:512 * (g_ + 1)],
                T[f"ost{ob_}"][:]))(h, g, ob),
                reads=[f"ost{ob}"], writes=[f"osp{h}"])
            yield

    def drain(gen):
        if gen is not None:
            for _ in gen:
                pass

    def emit_attention():
        emit_ln(with_flat=False)
        emit_qkv([24, 25, 26, 27])
        emit_collective()
        emit_qkv(list(range(24)))
        s.barrier()
        dma('gpsimd', lambda e: e.dma_start(T["kcos"][:], d_kcos), writes=["kcos"])
        dma('gpsimd', lambda e: e.dma_start(T["ksin"][:], d_ksin), writes=["ksin"])
        dma('gpsimd', lambda e: e.dma_start(T["qcos"][:], d_qcos), writes=["qcos"])
        dma('gpsimd', lambda e: e.dma_start(T["qsin"][:], d_qsin), writes=["qsin"])
        dma('sync', lambda e: e.dma_start(T["maskf"][:], d_mask), writes=["maskf"])
        dma('sync', lambda e: e.dma_start(
            T["maskt"][:].rearrange("p (r o) -> p r o", o=128),
            d_maskt.rearrange("r p o -> p r o")), writes=["maskt"])
        pending = None
        for i, kvh in enumerate(dbg.get("kvh_list", (0, 1, 2, 3))):
            drain(pending)
            pending = None
            emit_kv(kvh, i % 2)
            for qi in range(4):
                h = 4 * kvh + qi
                if qi == 0:
                    emit_qload(h, h % 2)
                p1 = p1_gen(h, h % 2)
                psteps = list(pending) if False else pending
                nsub = 8
                for si_ in range(nsub):
                    if next(p1, 'done') == 'done':
                        break
                    if pending is not None:
                        budget = 8 if si_ < nsub - 1 else 10 ** 9
                        for _ in range(budget):
                            if next(pending, 'done') == 'done':
                                pending = None
                                break
                for _ in p1:
                    pass
                drain(pending)
                pending = p2_gen(h, h % 2)
                if qi < 3:
                    emit_qload(4 * kvh + qi + 1, (h + 1) % 2)
        drain(pending)
        s.barrier()
        if dbg.get("do_ffn2", True):
            ffn_const_dmas()
        computed = {4 * kvh + qi for kvh in dbg.get("kvh_list", (0, 1, 2, 3))
                    for qi in range(4)}
        for kb in range(16):
            if kb in computed:
                dma('sync', (lambda kb_: lambda e: e.dma_start(
                    T[f"Ores{kb_}"][:], d_ospill[128 * kb_:128 * (kb_ + 1), :]))(kb),
                    reads=[f"osp{kb}"], writes=[f"Ores{kb}"])
            else:
                op('vector', (lambda kb_: lambda e: e.memset(
                    T[f"Ores{kb_}"][:], 0.0))(kb), writes=[f"Ores{kb}"])
        woslot = [0]
        for dm in range(DC):
            b = woslot[0]
            woslot[0] = 1 - b
            dma('gpsimd', (lambda dm_, b_: lambda e: e.dma_start(
                T[f"wos{b_}"][:].rearrange("p (a c) -> p a c", c=128),
                d_wo[dm_]))(dm, b),
                writes=[f"wos{b}"])
            pb = dm % 2
            hp = P[f'hp{pb}']
            for kb in range(16):
                for half in range(2):
                    sl = slice(512 * half, 512 * (half + 1))
                    op('tensor', (lambda b_, kb_, sl_, hp_: lambda e: e.matmul(
                        hp_[:, sl_], T[f"wos{b_}"][:, 128 * kb_:128 * (kb_ + 1)],
                        T[f"Ores{kb_}"][:, sl_],
                        start=(kb_ == 0), stop=(kb_ == 15)))(b, kb, sl, hp),
                        reads=[f"wos{b}", f"Ores{kb}"], writes=[hpk(pb, half)],
                        acc=acctag(kb, 16))
            for half in range(2):
                sl = slice(512 * half, 512 * (half + 1))
                op('vector', (lambda dm_, sl_, hp_: lambda e: e.tensor_add(
                    T[f"x{dm_}"][:, sl_], T[f"x{dm_}"][:, sl_], hp_[:, sl_]))(
                        dm, sl, P[f'hp{pb}']),
                    reads=[hpk(pb, half), f"x{dm}"], writes=[f"x{dm}"])

    emit_ffn()
    s.barrier()
    if dbg.get("do_attn", True):
        emit_attention()
        s.barrier()
    if dbg.get("do_ffn2", True):
        emit_ffn(skip_const_dmas=dbg.get("do_attn", True))
        s.barrier()
    for i in range(DC):
        dma('sync', (lambda i_: lambda e: e.dma_start(
            d_out[128 * i_:128 * (i_ + 1), :], T[f"x{i_}"][:]))(i),
            reads=[f"x{i}"], writes=[f"out{i}"])
    s.finalize()

    with nc.Block() as block:
        s.run(block)
    return nc


def _tile_lhsT(W, nb):
    K, N = W.shape
    return np.ascontiguousarray(
        W.reshape(K // 128, 128, N // nb, nb).transpose(2, 1, 0, 3))


def _rope_tables(pos):
    inv = (1.0 / (10000.0 ** (np.arange(0, D_ROPE, 2, dtype=np.float32)
                              / D_ROPE))).astype(np.float32)
    f = np.outer(pos.astype(np.float32), inv)
    emb = np.concatenate([f, f], axis=1)
    cos = np.cos(emb).T.astype(np.float32)
    sin = np.sin(emb).T.astype(np.float32)
    sinm = np.concatenate([-sin[0:32], sin[32:64]], axis=0)
    return np.ascontiguousarray(cos), np.ascontiguousarray(sinm)


def prep_inputs(inputs):
    wqkv = np.asarray(inputs["L0_attn_Wqkv"], np.float32)
    wkup = np.asarray(inputs["L0_attn_Wk_up"], np.float32)
    wvup = np.asarray(inputs["L0_attn_Wv_up"], np.float32)
    wo = np.asarray(inputs["L0_attn_Wo"], np.float32)
    sw1 = np.asarray(inputs["L0_ffn_shared_w1"], np.float32)
    sb1 = np.asarray(inputs["L0_ffn_shared_b1"], np.float32)
    sw2 = np.asarray(inputs["L0_ffn_shared_w2"], np.float32)
    sb2 = np.asarray(inputs["L0_ffn_shared_b2"], np.float32)
    rw1 = np.asarray(inputs["L0_ffn_routed_w1"], np.float32)
    rb1 = np.asarray(inputs["L0_ffn_routed_b1"], np.float32)
    rw2 = np.asarray(inputs["L0_ffn_routed_w2"], np.float32)
    rb2 = np.asarray(inputs["L0_ffn_routed_b2"], np.float32)
    cent = np.asarray(inputs["L0_ffn_centroids"], np.float32)
    gbias = np.asarray(inputs["L0_ffn_bias"], np.float32)
    x = np.asarray(inputs["L0_attn_hidden_states"], np.float32)

    shared = {}
    shared["wqkv"] = _tile_lhsT(wqkv, 128)
    shared["wk"] = np.ascontiguousarray(np.stack([
        wkup[:, 192 * h:192 * (h + 1)].reshape(4, 128, 192).transpose(1, 0, 2)
        for h in range(NKV)]))
    shared["wv"] = np.ascontiguousarray(np.stack([
        wvup[:, 128 * h:128 * (h + 1)].reshape(4, 128, 128).transpose(1, 0, 2)
        for h in range(NKV)]))
    shared["wo"] = _tile_lhsT(wo, 128)
    w1_all = np.concatenate([sw1, rw1], axis=0)
    w2_all = np.concatenate([sw2, rw2], axis=0)
    shared["w1"] = np.ascontiguousarray(
        np.stack([_tile_lhsT(w1_all[e], 128) for e in range(NE)]))
    shared["w2"] = np.ascontiguousarray(
        np.stack([_tile_lhsT(w2_all[e], 128) for e in range(NE)]))
    b1_all = np.concatenate([sb1, rb1], axis=0)
    shared["b1"] = np.ascontiguousarray(
        b1_all.reshape(NE, FC, 128).transpose(2, 0, 1))
    shared["b2sh"] = np.ascontiguousarray(sb2)
    shared["b2r"] = np.ascontiguousarray(rb2)
    shared["cent"] = np.ascontiguousarray(
        cent.T.reshape(DC, 128, NR).transpose(1, 0, 2))
    shared["gbias"] = np.ascontiguousarray(gbias[None, :])
    shared["ident"] = np.eye(128, dtype=np.float32)
    kcos, ksin = _rope_tables(np.arange(S))
    shared["kcos"], shared["ksin"] = kcos, ksin

    xr = x.reshape(B, 32, 128, D)
    in_maps = []
    for c in range(8):
        b, j = c // 4, c % 4
        tiles = [j + 4 * k for k in range(8)]
        xc = xr[b, tiles].reshape(TOK, D)
        m = dict(shared)
        m["x"] = np.ascontiguousarray(xc.T)
        pos = np.concatenate([np.arange(128 * t, 128 * (t + 1)) for t in tiles])
        qcos, qsin = _rope_tables(pos)
        m["qcos"], m["qsin"] = qcos, qsin
        mask = np.full((128, 512), NEG, np.float32)
        ii = np.arange(128)[:, None]
        jj = np.arange(512)[None, :]
        mask[jj <= 128 * j + ii] = 0.0
        m["mask"] = mask
        maskt = np.zeros((4, 128, 128), np.float32)
        for r in range(4):
            kk = 128 * r + np.arange(128)[:, None]
            qq = 128 * j + np.arange(128)[None, :]
            maskt[r][kk > qq] = NEG
        m["maskt"] = maskt
        in_maps.append(m)
    return in_maps


def kernel(**inputs):
    in_maps = prep_inputs(inputs)
    nc = build_nc(DEBUG)
    res = run_bass_kernel_spmd(nc, in_maps, core_ids=list(range(8)))
    out = np.zeros((B, 32, 128, D), np.float32)
    for c in range(8):
        b, j = c // 4, c % 4
        tiles = [j + 4 * k for k in range(8)]
        oc = np.asarray(res.results[c]["out"], np.float32).T.reshape(8, 128, D)
        for ki, t in enumerate(tiles):
            out[b, t] = oc[ki]
    return out.reshape(B, S, D)


if __name__ == "__main__":
    print("building program ...")
    nc = build_nc(DEBUG)
    print("instructions:", nc.n_instructions())
